# version 06
# speedup vs baseline: 1.3578x; 1.0404x over previous
import numpy as np
import ml_dtypes
from contextlib import ExitStack

import concourse.bass as bass
import concourse.mybir as mybir
from concourse.bass_utils import run_bass_kernel_spmd

F32 = mybir.dt.float32
BF16 = mybir.dt.bfloat16
AF = mybir.ActivationFunctionType
ALU = mybir.AluOpType
AX = mybir.AxisListType
BF_NP = ml_dtypes.bfloat16

D = 1024
KV = 1024
NE = 8
LN_EPS = 1e-5
L2_EPS = 1e-6
EXP_SHIFT = -24.0
QB = 512


class Sem:
    def __init__(self, handle, name):
        self.h = handle
        self.name = name
        self.v = 0

    def inc(self, n=1):
        self.v += n
        return self.v


class KB:
    def __init__(self, T=4096, debug=False, pa_stages=4, hg_stages=9):
        assert T % 1024 == 0
        self.T = T
        self.pa_stages = pa_stages
        self.hg_stages = hg_stages
        self.debug = debug
        self.nc = bass.Bass("TRN2", target_bir_lowering=False, debug=False)
        self.ops = {e: [] for e in ("tensor", "scalar", "vector", "gpsimd", "sync")}
        self.dbg = {}

    def emit(self, eng, fn):
        self.ops[eng].append(fn)

    def wait(self, eng, sem, val):
        if val > 0:
            self.emit(eng, lambda e, s=sem.h, v=val: e.wait_ge(s, v))

    def dma(self, eng, sem, out, in_):
        if sem.name == "sdbg" and sem.v > 0:
            self.wait(eng, sem, sem.v)
        tgt = sem.inc(16)
        self.emit(eng, lambda e, o=out, i=in_, s=sem.h: e.dma_start(out=o, in_=i).then_inc(s, 16))
        return tgt

    def mm(self, out, lhsT, rhs, start, stop, sem=None):
        tgt = sem.inc(1) if sem is not None else None
        def f(t, o=out, l=lhsT, rr=rhs, st=start, sp=stop, s=sem):
            i = t.matmul(o, l, rr, start=st, stop=sp)
            if s is not None:
                i.then_inc(s.h, 1)
        self.emit("tensor", f)
        return tgt

    def tr(self, out, in_, ident, sem=None):
        tgt = sem.inc(1) if sem is not None else None
        def f(t, o=out, i=in_, idt=ident, s=sem):
            ins = t.transpose(o, i, idt)
            if s is not None:
                ins.then_inc(s.h, 1)
        self.emit("tensor", f)
        return tgt

    def act(self, out, in_, func, bias=0.0, scale=1.0, sem=None):
        tgt = sem.inc(1) if sem is not None else None
        def f(s_, o=out, i=in_, fu=func, b=bias, sc=scale, s=sem):
            ins = s_.activation(o, i, fu, bias=b, scale=sc)
            if s is not None:
                ins.then_inc(s.h, 1)
        self.emit("scalar", f)
        return tgt

    def scopy(self, out, in_, sem=None):
        tgt = sem.inc(1) if sem is not None else None
        def f(s_, o=out, i=in_, s=sem):
            ins = s_.copy(o, i)
            if s is not None:
                ins.then_inc(s.h, 1)
        self.emit("scalar", f)
        return tgt

    def v(self, meth, *args, sem=None, **kw):
        tgt = sem.inc(1) if sem is not None else None
        def f(vv, m=meth, a=args, k=kw, s=sem):
            ins = getattr(vv, m)(*a, **k)
            if s is not None:
                ins.then_inc(s.h, 1)
        self.emit("vector", f)
        return tgt


    def vsp_op(self):
        self.v("tensor_copy", self.kc[:, 6:7], self.kc[:, 5:6])

    def dma_fm(self, eng, sem, sbuf2d, dram2d, a, c):
        o3 = sbuf2d.rearrange("p (a c) -> p a c", c=c)
        i3 = dram2d.rearrange("(a p) c -> p a c", p=128)
        return self.dma(eng, sem, o3, i3)

    def dma_fm_out(self, eng, sem, dram2d, sbuf2d, a, c):
        o3 = dram2d.rearrange("(a p) c -> p a c", p=128)
        i3 = sbuf2d.rearrange("p (a c) -> p a c", c=c)
        return self.dma(eng, sem, o3, i3)

    def bar(self, queues=False):
        vals = (self.sm.v, self.ss.v, self.sv.v)
        engs = ("tensor", "scalar", "vector") + (("sync", "gpsimd") if queues else ())
        for eng in engs:
            self.wait(eng, self.sm, vals[0])
            self.wait(eng, self.ss, vals[1])
            self.wait(eng, self.sv, vals[2])
        return vals

    def build(self):
        T = self.T
        self.NT = T // 128
        self.NQB = T // QB
        self.NCH = T // 128
        self._declare_dram()
        with ExitStack() as stk:
            self._alloc_onchip(stk)
            self._phase_init()
            self._phase_p0()
            self._phase_pa()
            self._phase_pa2()
            self._phase_pb1()
            self._phase_pb2a()
            self._phase_pb2b()
            self._phase_pc()
            final_sems = [self.so_yg, self.sdbg]
            for base in ("so_s", "so_xc", "so_q", "so_ctx", "so_x1", "so_out"):
                final_sems += getattr(self, base)
            for sem in final_sems:
                self.wait("sync", sem, sem.v)
            blk = stk.enter_context(self.nc.Block())
            blk.tensor(lambda t: [f(t) for f in self.ops["tensor"]])
            blk.scalar(lambda s: [f(s) for f in self.ops["scalar"]])
            blk.vector(lambda v: [f(v) for f in self.ops["vector"]])
            blk.gpsimd(lambda g: [f(g) for f in self.ops["gpsimd"]])
            blk.sync(lambda s: [f(s) for f in self.ops["sync"]])
        return self.nc

    def _declare_dram(self):
        nc, T = self.nc, self.T
        ap = lambda n, sh, dt, k: nc.dram_tensor(n, sh, dt, kind=k).ap()
        I, O, S = "ExternalInput", "ExternalOutput", "Internal"
        self.d_x = ap("x", [T, D], F32, I)
        self.d_W_in = ap("W_in", [D, 4 * KV], F32, I)
        self.d_W_ba = ap("W_ba", [D, 32], F32, I)
        self.d_cw_attn = ap("cw_attn", [24, 128, 4], F32, I)
        self.d_dtb = ap("dtb", [16, 1], F32, I)
        self.d_dtb2 = ap("dtb2", [128, 16], F32, I)
        self.d_Alog = ap("Alog", [16, 1], F32, I)
        self.d_Alog2 = ap("Alog2", [128, 16], F32, I)
        self.d_normw = ap("normw", [128, 1], F32, I)
        self.d_W_o = ap("W_o", [D, D], F32, I)
        self.d_cwf = ap("cwf", [3, D, D], F32, I)
        self.d_cbf = ap("cbf", [128, 8], F32, I)
        self.d_wqkvT = ap("wqkvT", [D, 3 * D], F32, I)
        self.d_bqkv = ap("bqkv", [128, 24], F32, I)
        self.d_woT = ap("woT", [D, D], F32, I)
        self.d_bo = ap("bo", [128, 8], F32, I)
        self.d_ln1g = ap("ln1g", [128, 8], F32, I)
        self.d_ln1b = ap("ln1b", [128, 8], F32, I)
        self.d_ln2g = ap("ln2g", [128, 8], F32, I)
        self.d_ln2b = ap("ln2b", [128, 8], F32, I)
        self.d_gatew = ap("gatew", [D, 8], F32, I)
        self.d_gateb = ap("gateb", [8, 1], F32, I)
        self.d_w1 = ap("w1", [NE, 32, 128, 8, 128], F32, I)
        self.d_b1 = ap("b1", [NE, 128, 32], F32, I)
        self.d_w2 = ap("w2", [NE, 32, 2, 128, 512], F32, I)
        self.d_b2 = ap("b2", [NE, D], F32, I)
        self.d_identb = ap("identb", [128, 128], BF16, I)
        self.d_identf = ap("identf", [128, 128], F32, I)
        self.d_maskS = ap("maskS", [128, 128], F32, I)
        self.d_maskI = ap("maskI", [128, 128], F32, I)
        self.d_maskST = ap("maskST", [128, 128], F32, I)
        self.d_sel16 = ap("sel16", [16, 16 * 128], F32, I)
        self.d_sel8b = ap("sel8b", [8, 8 * 128], BF16, I)
        self.d_triu = ap("triu", [128, 128], F32, I)
        self.d_e2 = ap("e2", [128, 2], BF16, I)
        self.d_sel2 = ap("sel2", [2, 128], BF16, I)
        self.d_onesb = ap("onesb", [128, 128], BF16, I)
        self.d_out = ap("out", [T, D], F32, O)
        self.d_yg = ap("scr_yg", [KV, T], BF16, S)
        self.d_sS = ap("scr_s", [D, T], F32, S)
        self.d_xcB = ap("scr_xcB", [D, T], BF16, S)
        self.d_qB = ap("scr_qB", [D, T], BF16, S)
        self.d_ctxB = ap("scr_ctxB", [D, T], BF16, S)
        self.d_x1F = ap("scr_x1F", [D, T], F32, S)
        self.d_x1B = ap("scr_x1B", [D, T], BF16, S)
        if self.debug:
            for n, sh, dt in [("xnT", [D, T], BF16), ("qkvc", [3 * KV, T], BF16),
                              ("z", [KV, T], BF16), ("gT", [16, T], F32),
                              ("lbT", [16, T], F32), ("cgT", [16, T], F32),
                              ("yg", [KV, T], BF16), ("probs", [8, T], BF16),
                              ("sT", [D, T], F32), ("xcT", [D, T], BF16),
                              ("ctxT", [D, T], BF16), ("x1T", [D, T], F32)]:
                self.dbg[n] = ap("dbg_" + n, sh, dt, "ExternalOutput")

    def _alloc_onchip(self, stk):
        nc, T = self.nc, self.T
        E = stk.enter_context
        sb = lambda n, sh, dt: E(nc.sbuf_tensor(n, sh, dt))
        self.A = sb("poolA", [128, 36864], BF16)
        self.B = sb("poolB", [128, 36864], BF16)
        self.Cp = sb("poolC", [128, 12288], BF16)
        self.Dp = sb("poolD", [128, 3584], F32)
        self.identb = sb("identb_s", [128, 128], BF16)
        self.identf = sb("identf_s", [128, 128], F32)
        self.maskS = sb("maskS_s", [128, 128], F32)
        self.maskI = sb("maskI_s", [128, 128], F32)
        self.maskST = sb("maskST_s", [128, 128], F32)
        self.sel16 = sb("sel16_s", [16, 2048], F32)
        self.triu = sb("triu_s", [128, 128], F32)
        self.e2 = sb("e2_s", [128, 2], BF16)
        self.sel2 = sb("sel2_s", [2, 128], BF16)
        self.onesb = sb("onesb_s", [128, 128], BF16)
        self.kc = sb("kconst", [128, 8], F32)
        self.dtb = sb("dtb_s", [16, 1], F32)
        self.dtb2 = sb("dtb2_s", [128, 16], F32)
        self.Alog = sb("Alog_s", [16, 1], F32)
        self.Alog2 = sb("Alog2_s", [128, 16], F32)
        self.Aneg1 = sb("Aneg1_s", [16, 1], F32)
        self.Aneg2 = sb("Aneg2_s", [128, 16], F32)
        self.normw = sb("normw_s", [128, 1], F32)
        self.cbf = sb("cbf_s", [128, 8], F32)
        self.bqkv = sb("bqkv_s", [128, 24], F32)
        self.bo = sb("bo_s", [128, 8], F32)
        self.ln1g = sb("ln1g_s", [128, 8], F32)
        self.ln1b = sb("ln1b_s", [128, 8], F32)
        self.ln2g = sb("ln2g_s", [128, 8], F32)
        self.ln2b = sb("ln2b_s", [128, 8], F32)
        self.gatew = sb("gatew_s", [128, 64], BF16)
        self.gateb = sb("gateb_s", [8, 1], F32)
        self.b1sb = sb("b1_s", [128, 64], F32)
        self.probsT = self.A[0:8, 8 * T:9 * T]
        self.sel8b = self.B[0:8, 33280:34304]
        self.b2sb = self.B[0:8, 34304:35328]
        self.Wba = sb("Wba_s", [128, 256], BF16)
        self.cwattn = sb("cwattn_s", [128, 96], F32)
        self.gT = sb("gT_s", [16, 128], F32)
        self.lbT = sb("lbT_s", [16, 128], F32)
        self.cgT = sb("cgT_s", [16, 128], F32)
        self.aT2 = sb("aT2_s", [16, 128], F32)
        self.rowpair = sb("rowpair_s", [16, 256], F32)
        self.khrow = sb("khrow_s", [16, 129], F32)
        self.gsum = sb("gsum_s", [16, max(self.T // 128, 2)], F32)
        self.cg2 = sb("cg2_s", [128, 16], F32)
        self.bcol = sb("bcol_s", [128, 16], F32)
        self.cblcol = sb("cblcol_s", [128, 16], F32)
        self.S0f = sb("S0f_s", [64, 16 * 64], F32)
        self.S0b = sb("S0b_s", [64, 16 * 64], BF16)
        self.PSA = E(nc.psum_tensor("PSA", [128, 2048], F32))
        self.PSB = E(nc.psum_tensor("PSB", [128, 2048], F32))
        names = ["qc", "qwin", "qwo", "qcwf", "qwqkv", "qw1", "qw2",
                 "qb1", "qq", "qctx", "qx1b", "so_yg", "sdbg", "sm", "ss", "sv"]
        for base in ("qx", "qyg", "qxc", "qx1f", "qss",
                     "so_s", "so_xc", "so_q", "so_ctx", "so_x1", "so_out"):
            names += [base + "_0", base + "_1"]
        for nm in names:
            setattr(self, nm, Sem(E(nc.semaphore(nm)), nm))
        for base in ("qx", "qyg", "qxc", "qx1f", "qss",
                     "so_s", "so_xc", "so_q", "so_ctx", "so_x1", "so_out"):
            setattr(self, base, [getattr(self, base + "_0"), getattr(self, base + "_1")])

    def _phase_init(self):
        qh, qg, sv, ss = self.qc, self.qwin, self.sv, self.ss
        for dst, src in [(self.identb, self.d_identb), (self.identf, self.d_identf),
                         (self.maskS, self.d_maskS), (self.maskI, self.d_maskI),
                         (self.maskST, self.d_maskST), (self.sel16, self.d_sel16),
                         (self.triu, self.d_triu),
                         (self.e2, self.d_e2), (self.sel2, self.d_sel2),
                         (self.onesb, self.d_onesb),
                         (self.dtb, self.d_dtb), (self.dtb2, self.d_dtb2),
                         (self.Alog, self.d_Alog), (self.Alog2, self.d_Alog2),
                         (self.normw, self.d_normw),
                         (self.cbf, self.d_cbf), (self.bqkv, self.d_bqkv),
                         (self.bo, self.d_bo), (self.ln1g, self.d_ln1g),
                         (self.ln1b, self.d_ln1b), (self.ln2g, self.d_ln2g),
                         (self.ln2b, self.d_ln2b), (self.gateb, self.d_gateb)]:
            self.dma("sync", qh, dst[:, :], src[:, :])
        self.dma("sync", qh, self.cwattn[:, :].rearrange("p (a c) -> p a c", c=4),
                 self.d_cw_attn.rearrange("a p c -> p a c"))
        self.qh_init = qh.v
        self.dma_fm("gpsimd", qg, self.gatew[:, :], self.d_gatew, 8, 8)
        self.qg_init = self.dma_fm("gpsimd", qg, self.Wba[:, :], self.d_W_ba, 8, 32)
        kc = self.kc
        for col, val in [(0, LN_EPS), (1, L2_EPS), (2, EXP_SHIFT),
                         (3, 1.0 / 64.0), (4, 1.0 / 1024.0), (5, 1.0),
                         (7, -1.0)]:
            self.v("memset", kc[:, col:col + 1], val)
        self.wait("vector", qh, self.qh_init)
        self.v("tensor_scalar_max", self.Aneg1[:, :], self.Alog[:, :], -2.3)
        v0 = self.v("tensor_scalar_max", self.Aneg2[:, :], self.Alog2[:, :], -2.3, sem=sv)
        self.wait("scalar", sv, v0)
        self.act(self.Aneg1[:, :], self.Aneg1[:, :], AF.Exp)
        s0 = self.act(self.Aneg2[:, :], self.Aneg2[:, :], AF.Exp, sem=ss)
        self.wait("vector", ss, s0)
        self.v("tensor_scalar_mul", self.Aneg1[:, :], self.Aneg1[:, :], -1.0)
        self.init_v = self.v("tensor_scalar_mul", self.Aneg2[:, :], self.Aneg2[:, :], -1.0,
                             sem=sv)

    def _phase_p0(self):
        T, NT = self.T, self.NT
        sm, ss, sv = self.sm, self.ss, self.sv
        Dp, Cp = self.Dp, self.Cp
        self.xnT = self.A
        evacs = []
        xn_done_v = []
        mm_done = []
        for i in range(NT):
            buf = i % 2
            xs = Dp[:, buf * 1024:(buf + 1) * 1024]
            xn_s = Cp[:, buf * 1024:(buf + 1) * 1024]
            if i >= 2:
                self.wait("sync", sv, xn_done_v[i - 2])
            ld = self.dma("sync", self.qx[i % 2], xs, self.d_x[i * 128:(i + 1) * 128, :])
            self.wait("vector", self.qx[i % 2], ld)
            s_mean = Dp[:, 2048 + buf * 4: 2049 + buf * 4]
            s_ssq = Dp[:, 2049 + buf * 4: 2050 + buf * 4]
            s_tmp = Dp[:, 2050 + buf * 4: 2051 + buf * 4]
            s_rstd = Dp[:, 2051 + buf * 4: 2052 + buf * 4]
            sqb = Dp[:, 2304:3328]
            self.v("reduce_sum", s_mean, xs, axis=AX.X)
            self.v("tensor_mul", sqb, xs, xs)
            self.v("tensor_scalar_mul", s_mean, s_mean, 1.0 / 1024.0)
            self.v("reduce_sum", s_ssq, sqb, axis=AX.X)
            self.v("tensor_mul", s_tmp, s_mean, s_mean)
            self.vsp_op()
            vv = self.v("scalar_tensor_tensor", s_ssq, s_ssq, self.kc[:, 4:5],
                        s_tmp, op0=ALU.mult, op1=ALU.subtract, sem=sv)
            self.wait("scalar", sv, vv)
            self.act(s_rstd, s_ssq, AF.Ln, bias=self.kc[:, 0:1])
            sr = self.act(s_rstd, s_rstd, AF.Exp, scale=-0.5, sem=ss)
            self.wait("vector", ss, sr)
            if i >= 2:
                self.wait("vector", sm, mm_done[i - 2])
            vx = self.v("tensor_scalar", xn_s, xs, s_mean, s_rstd,
                        op0=ALU.subtract, op1=ALU.mult, sem=sv)
            xn_done_v.append(vx)
            self.wait("tensor", sv, vx)
            for kt in range(8):
                slot = i * 8 + kt
                pslot = self.PSB[:, (slot % 4) * 512:(slot % 4) * 512 + 64].bitcast(BF16)
                if slot >= 4:
                    self.wait("tensor", ss, evacs[slot - 4])
                mmv = self.tr(pslot, xn_s[:, kt * 128:(kt + 1) * 128], self.identb[:, :], sem=sm)
                self.wait("scalar", sm, mmv)
                evacs.append(self.scopy(self.xnT[:, kt * T + i * 128: kt * T + (i + 1) * 128],
                                        pslot, sem=ss))
            mm_done.append(sm.v)
        self.p0_ss = ss.v
        self.p0_sm = sm.v
        if self.debug:
            self.wait("sync", ss, self.p0_ss)
            for kt in range(8):
                self.dma("sync", self.sdbg, self.dbg["xnT"][kt * 128:(kt + 1) * 128, :],
                         self.xnT[:, kt * T:(kt + 1) * T])

    def _phase_pa(self):
        T, NT, NCH = self.T, self.NT, self.NCH
        qh, qg, qs = self.sdbg, self.qwin, self.so_yg
        sm, ss, sv = self.sm, self.ss, self.sv
        A, B, Cp, Dp = self.A, self.B, self.Cp, self.Dp
        PSA, PSB = self.PSA, self.PSB
        xnT = self.xnT

        g_rep = A[0:64, 32768:33792]
        kh_rep = A[0:64, 33792:34816]
        kbar_w = A[0:64, 34816:35840]
        qbar_w = A[0:64, 35840:36864]
        W_in_sb = B[:, 0:32768]
        P_a = B[:, 32768:33792]
        P_b = B[:, 33792:34816]
        PT_a = B[:, 34816:35840]
        PT_b = B[:, 35840:36864]
        rawq = Cp[:, 0:3144]
        zreg = Cp[:, 3144:4168]
        qkvc = Cp[:, 4168:7240]
        yreg = Cp[:, 7240:8264]
        Aqk = Cp[:, 8264:9288]
        Rbuf = Cp[:, 9288:9800]
        Ubuf = Cp[:, 9800:10312]
        ysq = Cp[:, 10312:11336]
        WS = Dp[:, 0:1024]
        ba2 = Dp[:, 1024:1056]
        g2c = Dp[:, 1056:1072]
        sp2 = Dp[:, 1072:1088]
        tmp2 = Dp[:, 1088:1104]
        gCc = Dp[0:64, 1104:1112]
        rmsrow = Dp[0:2, 1112:2136]
        rmsrow_b = Dp[0:2, 2136:2648].bitcast(BF16)
        e_wide = Dp[:, 2136:2648].bitcast(BF16)
        khat_w = Dp[0:64, 2648:3160].bitcast(BF16)
        khT_w = Dp[:, 3160:3416].bitcast(BF16)
        onesf = Dp[0:16, 3416:3544]
        vt_wide = Cp[:, 11336:11848]

        ld_win = 0
        for kt in range(8):
            ld_win = self.dma("gpsimd", qg, W_in_sb[:, kt * 4096:(kt + 1) * 4096],
                              self.d_W_in[kt * 128:(kt + 1) * 128, :])
        raw3 = rawq.rearrange("p (a c) -> p a c", c=131)
        self.wait("vector", self.sm, self.p0_sm)
        self.v("memset", onesf, 1.0)
        self.v("memset", raw3[:, :, 0:3], 0.0)
        self.v("memset", self.S0f[:, :], 0.0)
        pa_pre = self.v("memset", self.S0b[:, :], 0.0, sem=sv)

        prod_ev = []
        yg_store = {}

        for c in range(NCH):
            c0 = c * 128
            self.bar()
            if c == 0:
                self.wait("tensor", sv, pa_pre)
                self.wait("tensor", ss, self.p0_ss)
                self.wait("tensor", qg, ld_win)
                self.wait("tensor", sv, self.init_v)
                self.wait("scalar", sv, pa_pre)
            if c >= 1 and (c - 1) in yg_store:
                self.wait("vector", qs, yg_store[c - 1])

            slots = [PSB[:, 0:128], PSB[:, 512:640]]
            for m in range(32):
                sl = slots[m % 2]
                gi = c * 32 + m
                if gi >= 2:
                    self.wait("tensor", ss, prod_ev[gi - 2])
                mmv = None
                for kt in range(8):
                    mmv = self.mm(sl, W_in_sb[:, kt * 4096 + m * 128:kt * 4096 + (m + 1) * 128],
                                  xnT[:, kt * T + c0:kt * T + c0 + 128],
                                  start=(kt == 0), stop=(kt == 7),
                                  sem=(sm if kt == 7 else None))
                self.wait("scalar", sm, mmv)
                if m < 24:
                    dst = raw3[:, m, 3:131]
                else:
                    dst = zreg[:, (m - 24) * 128:(m - 23) * 128]
                prod_ev.append(self.scopy(dst, sl, sem=ss))
            prod_done = ss.v

            if self.pa_stages < 1:
                continue
            bslot = PSB[:, 1152:1280]
            bslot2 = PSB[:, 1280:1408]
            bslot3 = PSB[:, 1408:1440]
            bslot4 = PSB[:, 1440:1456]
            for off, sl in ((0, bslot), (16, bslot2)):
                for kt in range(8):
                    self.mm(sl[0:16, :], self.Wba[:, kt * 32 + off:kt * 32 + off + 16],
                            xnT[:, kt * T + c0:kt * T + c0 + 128],
                            start=(kt == 0), stop=(kt == 7),
                            sem=(sm if kt == 7 else None))
            for kt in range(8):
                self.mm(bslot3, xnT[:, kt * T + c0:kt * T + c0 + 128],
                        self.Wba[:, kt * 32:(kt + 1) * 32],
                        start=(kt == 0), stop=(kt == 7),
                        sem=(sm if kt == 7 else None))
            ba_mms = sm.v
            gT_c = self.gT[:, :]
            lbT_c = self.lbT[:, :]
            cgT_c = self.cgT[:, :]
            aT_c = self.aT2[:, :]
            self.wait("vector", sm, ba_mms)
            self.v("tensor_scalar_add", aT_c, bslot2[0:16, :], self.dtb[:, 0:1])
            self.vsp_op()
            vva = self.v("tensor_scalar_max", lbT_c, aT_c, 0.0, sem=sv)
            self.wait("scalar", sv, vva)
            self.act(gT_c, aT_c, AF.Abs)
            self.act(gT_c, gT_c, AF.Exp, scale=-1.0)
            sse = self.act(gT_c, gT_c, AF.Ln, bias=self.kc[0:16, 5:6], sem=ss)
            self.wait("vector", ss, sse)
            self.v("tensor_add", gT_c, gT_c, lbT_c)
            self.vsp_op()
            self.v("tensor_scalar_mul", gT_c, gT_c, self.Aneg1[:, 0:1])
            vvb = self.v("tensor_scalar_min", lbT_c, bslot[0:16, :], 0.0, sem=sv)
            self.wait("scalar", sv, vvb)
            s_abs_b = self.act(aT_c, bslot[0:16, :], AF.Abs, sem=ss)
            self.act(aT_c, aT_c, AF.Exp, scale=-1.0)
            ssb = self.act(aT_c, aT_c, AF.Ln, bias=self.kc[0:16, 5:6], sem=ss)
            self.wait("vector", ss, ssb)
            self.v("tensor_sub", lbT_c, lbT_c, aT_c)
            self.v("tensor_tensor_scan", cgT_c, onesf, gT_c, 0.0,
                   op0=ALU.mult, op1=ALU.add)
            gs = self.gsum[:, c:c + 1]
            self.v("reduce_sum", gs, gT_c, axis=AX.X)
            self.v("tensor_add", self.rowpair[:, 0:128], cgT_c, lbT_c)
            self.v("tensor_copy", self.rowpair[:, 128:256], cgT_c)
            self.v("tensor_scalar", self.khrow[:, 0:128], cgT_c, gs, -1.0,
                   op0=ALU.subtract, op1=ALU.mult)
            row_v = self.v("tensor_copy", self.khrow[:, 128:129], gs, sem=sv)
            self.v("tensor_copy", ba2, bslot3)
            bcol_c = self.bcol[:, :]
            cg2_c = self.cg2[:, :]
            cbl_c = self.cblcol[:, :]
            vc2 = self.v("tensor_copy", self.kc[:, 6:7], self.kc[:, 5:6], sem=sv)
            self.wait("scalar", sv, vc2)
            self.act(bcol_c, ba2[:, 0:16], AF.Sigmoid)
            self.v("tensor_add", sp2, ba2[:, 16:32], self.dtb2[:, :])
            self.vsp_op()
            vv2 = self.v("tensor_scalar_max", tmp2, sp2, 0.0, sem=sv)
            self.wait("scalar", sv, vv2)
            self.act(g2c, sp2, AF.Abs)
            self.act(g2c, g2c, AF.Exp, scale=-1.0)
            ss2 = self.act(g2c, g2c, AF.Ln, bias=self.kc[:, 5:6], sem=ss)
            self.wait("vector", ss, ss2)
            self.v("tensor_add", g2c, g2c, tmp2)
            self.vsp_op()
            v_g2 = self.v("tensor_mul", g2c, g2c, self.Aneg2[:, :], sem=sv)
            self.wait("tensor", sv, v_g2)
            self.wait("tensor", ss, s_abs_b)
            cmv = self.mm(bslot4, self.triu[:, :], g2c, start=True, stop=True, sem=sm)
            self.wait("scalar", sm, cmv)
            s_cg2 = self.scopy(cg2_c, bslot4, sem=ss)
            vv3 = self.v("tensor_scalar_min", tmp2, ba2[:, 0:16], 0.0, sem=sv)
            self.wait("scalar", sv, vv3)
            self.act(sp2, ba2[:, 0:16], AF.Abs)
            self.act(sp2, sp2, AF.Exp, scale=-1.0)
            ss3 = self.act(sp2, sp2, AF.Ln, bias=self.kc[:, 5:6], sem=ss)
            self.wait("vector", ss, ss3)
            self.v("tensor_sub", tmp2, tmp2, sp2)
            self.vsp_op()
            self.wait("vector", ss, s_cg2)
            v_cbl = self.v("tensor_add", cbl_c, cg2_c, tmp2, sem=sv)

            if self.pa_stages < 2:
                continue
            self.wait("vector", ss, prod_done)
            cq3 = qkvc.rearrange("p (a c) -> p a c", c=128)
            for mp in range(0, 24, 2):
                for tap in range(4):
                    for m in (mp, mp + 1):
                        wc = self.cwattn[:, m * 4 + tap:m * 4 + tap + 1]
                        if tap == 0:
                            self.v("tensor_scalar_mul", cq3[:, m:m + 1, :],
                                   raw3[:, m:m + 1, 0:128], wc)
                        else:
                            self.v("scalar_tensor_tensor", cq3[:, m:m + 1, :],
                                   raw3[:, m:m + 1, tap:tap + 128], wc,
                                   cq3[:, m:m + 1, :], op0=ALU.mult, op1=ALU.add)
            conv_v = self.v("tensor_copy", raw3[:, :, 0:3], raw3[:, :, 128:131], sem=sv)
            self.wait("scalar", sv, conv_v)
            silu_s = self.act(qkvc, qkvc, AF.Silu, sem=ss)

            rstd2 = Dp[0:2, 0:2048]
            rstd2b = Dp[0:2, 2048:3072].bitcast(BF16)
            sqtile = Dp[:, 3072:3136].bitcast(BF16)
            self.wait("vector", ss, silu_s)
            l2_mm = None
            for i in range(16):
                vq = self.v("tensor_mul", sqtile, qkvc[:, i * 128:(i + 1) * 128],
                            qkvc[:, i * 128:(i + 1) * 128], sem=sv)
                self.wait("tensor", sv, vq)
                l2_mm = self.mm(PSA[0:2, i * 128:(i + 1) * 128], self.e2[:, :], sqtile,
                                start=True, stop=True, sem=sm)
                self.wait("vector", sm, l2_mm)
            self.v("tensor_scalar_add", rstd2[0:2, 0:1024], PSA[0:2, 0:1024], L2_EPS)
            self.v("tensor_scalar_add", rstd2[0:2, 1024:2048], PSA[0:2, 1024:2048], L2_EPS)
            v_r2 = self.v("tensor_copy", self.kc[:, 6:7], self.kc[:, 5:6], sem=sv)
            self.wait("scalar", sv, v_r2)
            s_sq2 = self.act(rstd2, rstd2, AF.Sqrt, sem=ss)
            self.wait("vector", ss, s_sq2)
            self.v("reciprocal", rstd2[0:2, 0:1024], rstd2[0:2, 0:1024])
            self.v("reciprocal", rstd2[0:2, 1024:2048], rstd2[0:2, 1024:2048])
            self.v("tensor_scalar_mul", rstd2[0:2, 0:1024], rstd2[0:2, 0:1024], 0.125)
            self.v("tensor_copy", rstd2b[0:2, 1024:2048], rstd2[0:2, 1024:2048])
            v_r2b = self.v("tensor_copy", rstd2b[0:2, 0:1024], rstd2[0:2, 0:1024], sem=sv)
            self.wait("tensor", sv, v_r2b)
            l2mul = {}
            for i in range(16):
                if i >= 4:
                    self.wait("tensor", sv, l2mul[i - 4])
                bmm2 = self.mm(PSB[:, (i % 4) * 512:(i % 4) * 512 + 128], self.sel2[:, :],
                               rstd2b[0:2, i * 128:(i + 1) * 128], start=True, stop=True,
                               sem=sm)
                self.wait("vector", sm, bmm2)
                l2mul[i] = self.v("tensor_mul", qkvc[:, i * 128:(i + 1) * 128],
                                  qkvc[:, i * 128:(i + 1) * 128],
                                  PSB[:, (i % 4) * 512:(i % 4) * 512 + 128], sem=sv)
            l2_done = self.sv.v

            if self.debug:
                self.wait("sync", ss, silu_s)
                self.dma_fm_out("sync", qh, self.dbg["qkvc"][:, c0:c0 + 128], qkvc, 24, 128)
                self.dma_fm_out("sync", qh, self.dbg["z"][:, c0:c0 + 128], zreg, 8, 128)
                self.wait("sync", sv, row_v)
                self.dma("sync", qh, self.dbg["gT"][:, c0:c0 + 128], gT_c)
                self.dma("sync", qh, self.dbg["lbT"][:, c0:c0 + 128], lbT_c)
                self.dma("sync", qh, self.dbg["cgT"][:, c0:c0 + 128], cgT_c)

            if self.pa_stages < 3:
                continue
            for hg in range(2):
                h0 = hg * 8
                self.bar()
                R1 = PSA[:, 0:1024]
                R2 = PSA[:, 1024:2048]
                R1v = R1.rearrange("p (a c) -> p a c", c=128)
                R2v = R2.rearrange("p (a c) -> p a c", c=128)
                WS3 = WS.rearrange("p (a c) -> p a c", c=128)
                hperm = [0, 2, 4, 6, 1, 3, 5, 7]

                def hsl(j, h0_=h0):
                    return h0_ + hperm[j]

                def ksl(j, h0_=h0):
                    h = hsl(j, h0_)
                    return qkvc[64 * (h % 2):64 * (h % 2) + 64,
                                (8 + h // 2) * 128:(9 + h // 2) * 128]

                def qsl(j, h0_=h0):
                    h = hsl(j, h0_)
                    return qkvc[64 * (h % 2):64 * (h % 2) + 64,
                                (h // 2) * 128:(h // 2 + 1) * 128]

                def vsl(j, h0_=h0):
                    h = hsl(j, h0_)
                    return qkvc[64 * (h % 2):64 * (h % 2) + 64,
                                (16 + h // 2) * 128:(17 + h // 2) * 128]

                cg_e = self.cg2[:, h0:h0 + 8:2].unsqueeze(2).broadcast_to([128, 4, 128])
                cg_o = self.cg2[:, h0 + 1:h0 + 8:2].unsqueeze(2).broadcast_to([128, 4, 128])
                cbl_e = self.cblcol[:, h0:h0 + 8:2].unsqueeze(2).broadcast_to([128, 4, 128])
                cbl_o = self.cblcol[:, h0 + 1:h0 + 8:2].unsqueeze(2).broadcast_to([128, 4, 128])
                bc_e = self.bcol[:, h0:h0 + 8:2].unsqueeze(2).broadcast_to([128, 4, 64])
                bc_o = self.bcol[:, h0 + 1:h0 + 8:2].unsqueeze(2).broadcast_to([128, 4, 64])
                maskS_b = self.maskS[:, :].unsqueeze(1).broadcast_to([128, 4, 128])
                maskST_b = self.maskST[:, :].unsqueeze(1).broadcast_to([128, 4, 128])
                maskI_b = self.maskI[:, :].unsqueeze(1).broadcast_to([128, 4, 128])
                WSe = WS[:, 0:512].rearrange("p (a c) -> p a c", c=128)
                WSo = WS[:, 512:1024].rearrange("p (a c) -> p a c", c=128)
                R2e = R2[:, 0:512].rearrange("p (a c) -> p a c", c=128)
                R2o = R2[:, 512:1024].rearrange("p (a c) -> p a c", c=128)

                if hg == 0:
                    self.wait("tensor", sv, l2_done)
                for j in range(8):
                    self.mm(R1[:, j * 128:(j + 1) * 128], ksl(j), ksl(j),
                            start=True, stop=True, sem=(sm if j == 7 else None))
                kk_mm = sm.v
                for j in range(8):
                    self.mm(R2[:, j * 128:(j + 1) * 128],
                            self.sel16[:, hsl(j) * 128:(hsl(j) + 1) * 128],
                            self.rowpair[:, 0:128], start=True, stop=True,
                            sem=(sm if j == 7 else None))
                bn_mm = sm.v
                self.wait("vector", sm, bn_mm)
                self.v("tensor_tensor", WSe, R2e, cg_e, op=ALU.subtract)
                self.v("tensor_tensor", WSo, R2o, cg_o, op=ALU.subtract)
                self.v("tensor_tensor", WSe, WSe, maskS_b, op=ALU.add)
                wv = self.v("tensor_tensor", WSo, WSo, maskS_b, op=ALU.add, sem=sv)
                self.wait("scalar", sv, wv)
                we = self.act(e_wide, WS, AF.Exp, sem=ss)
                self.wait("vector", ss, we)
                self.wait("vector", sm, kk_mm)
                v_N = self.v("tensor_tensor", P_a, R1, e_wide, op=ALU.mult, sem=sv)
                self.wait("tensor", sv, wv)
                for j in range(8):
                    self.mm(R2[:, j * 128:(j + 1) * 128],
                            self.sel16[:, hsl(j) * 128:(hsl(j) + 1) * 128],
                            self.rowpair[:, 128:256], start=True, stop=True,
                            sem=(sm if j == 7 else None))
                bq_mm = sm.v
                self.wait("vector", sm, bq_mm)
                self.wait("vector", sv, v_cbl)
                self.v("tensor_tensor", WSe, cbl_e, R2e, op=ALU.subtract)
                self.v("tensor_tensor", WSo, cbl_o, R2o, op=ALU.subtract)
                self.v("tensor_tensor", WSe, WSe, maskST_b, op=ALU.add)
                wv2 = self.v("tensor_tensor", WSo, WSo, maskST_b, op=ALU.add, sem=sv)
                self.wait("scalar", sv, wv2)
                we2 = self.act(e_wide, WS, AF.Exp, sem=ss)
                self.wait("vector", ss, we2)
                v_NT = self.v("tensor_tensor", PT_a, R1, e_wide, op=ALU.mult, sem=sv)
                s_gr = self.act(g_rep, R2[0:64, :], AF.Exp, sem=ss)
                self.wait("tensor", sv, v_NT)
                for j in range(8):
                    self.mm(R1[:, j * 128:(j + 1) * 128], ksl(j), qsl(j),
                            start=True, stop=True, sem=(sm if j == 7 else None))
                qk_mm = sm.v
                self.wait("vector", sm, qk_mm)
                self.v("tensor_tensor", WSe, R2e, cg_e, op=ALU.subtract)
                self.v("tensor_tensor", WSo, R2o, cg_o, op=ALU.subtract)
                self.v("tensor_tensor", WSe, WSe, maskI_b, op=ALU.add)
                wv3 = self.v("tensor_tensor", WSo, WSo, maskI_b, op=ALU.add, sem=sv)
                self.wait("scalar", sv, wv3)
                we3 = self.act(e_wide, WS, AF.Exp, sem=ss)
                self.wait("vector", ss, we3)
                v_A = self.v("tensor_tensor", Aqk, R1, e_wide, op=ALU.mult, sem=sv)
                self.wait("tensor", sv, row_v)
                for j in range(8):
                    self.mm(PSB[0:64, j * 256:j * 256 + 129],
                            self.sel16[:, hsl(j) * 128:hsl(j) * 128 + 64],
                            self.khrow[:, 0:129], start=True, stop=True,
                            sem=(sm if j == 7 else None))
                kh_mm = sm.v
                self.wait("scalar", sm, kh_mm)
                kh3 = PSB[0:64, 0:2048].rearrange("p (a c) -> p a c", c=256)[:, :, 0:129]
                self.act(kh_rep.rearrange("p (a c) -> p a c", c=128),
                         kh3[:, :, 0:128], AF.Exp)
                s_gC = self.act(gCc, kh3[:, :, 128:129].squeeze(2), AF.Exp, sem=ss)
                self.wait("vector", ss, s_gr)
                self.wait("vector", ss, s_gC)
                v_bars = None
                ksrc_st = e_wide[0:64, 0:512]
                qsrc_st = e_wide[0:64, 512:1024]
                for par in range(2):
                    h_lo = h0 + par
                    t0 = h_lo // 2
                    src = qkvc[par * 64:par * 64 + 64,
                               (8 + t0) * 128:(8 + t0 + 4) * 128]
                    srcq = qkvc[par * 64:par * 64 + 64, t0 * 128:(t0 + 4) * 128]
                    if par == 1:
                        self.v("tensor_copy", ksrc_st, src)
                        self.v("tensor_copy", qsrc_st, srcq)
                        src, srcq = ksrc_st, qsrc_st
                    sl = slice(par * 512, par * 512 + 512)
                    gsl = g_rep[:, sl].rearrange("p (a c) -> p a c", c=128)
                    khsl = kh_rep[:, sl].rearrange("p (a c) -> p a c", c=128)
                    kdst = kbar_w[:, sl].rearrange("p (a c) -> p a c", c=128)
                    qdst = qbar_w[:, sl].rearrange("p (a c) -> p a c", c=128)
                    khdst = khat_w[:, sl].rearrange("p (a c) -> p a c", c=128)
                    src3 = src.rearrange("p (a c) -> p a c", c=128)
                    srcq3 = srcq.rearrange("p (a c) -> p a c", c=128)
                    self.v("tensor_tensor", kdst, src3, gsl, op=ALU.mult)
                    self.v("tensor_tensor", qdst, srcq3, gsl, op=ALU.mult)
                    v_bars = self.v("tensor_tensor", khdst, src3, khsl, op=ALU.mult,
                                    sem=(sv if par == 1 else None))
                self.wait("tensor", sv, v_bars)
                for j in range(8):
                    self.mm(PSB[:, 1152 + j * 64:1152 + (j + 1) * 64],
                            kbar_w[:, j * 128:(j + 1) * 128],
                            self.S0b[:, hsl(j) * 64:(hsl(j) + 1) * 64],
                            start=True, stop=True, sem=(sm if j == 7 else None))
                ws0_mm = sm.v
                for j in range(8):
                    par = j // 4
                    if par == 0:
                        ps = PSB[:, 1664 + j * 32:1664 + j * 32 + 32].bitcast(BF16)
                    else:
                        ps = PSA[:, (j - 4) * 32:(j - 4) * 32 + 32].bitcast(BF16)
                    idpar = hsl(j) % 2
                    self.tr(ps, vsl(j),
                            self.identb[idpar * 64:idpar * 64 + 64,
                                        idpar * 64:idpar * 64 + 64],
                            sem=(sm if j == 7 else None))
                vt_mm = sm.v
                self.wait("scalar", sm, vt_mm)
                self.scopy(vt_wide[:, 0:256], PSB[:, 1664:1792].bitcast(BF16))
                s_vt = self.scopy(vt_wide[:, 256:512], PSA[:, 0:128].bitcast(BF16), sem=ss)
                self.wait("vector", ss, s_vt)
                self.wait("vector", sm, ws0_mm)
                self.v("tensor_tensor", WS[:, 0:256], vt_wide[:, 0:256],
                       PSB[:, 1152:1408], op=ALU.subtract)
                self.v("tensor_tensor", WS[:, 256:512], vt_wide[:, 256:512],
                       PSB[:, 1408:1664], op=ALU.subtract)
                self.v("tensor_tensor",
                       Rbuf[:, 0:256].rearrange("p (a c) -> p a c", c=64),
                       WS[:, 0:256].rearrange("p (a c) -> p a c", c=64),
                       bc_e, op=ALU.mult)
                v_R = self.v("tensor_tensor",
                             Rbuf[:, 256:512].rearrange("p (a c) -> p a c", c=64),
                             WS[:, 256:512].rearrange("p (a c) -> p a c", c=64),
                             bc_o, op=ALU.mult, sem=sv)
                if self.hg_stages < 3:
                    continue
                psU = PSB[:, 1152:1664]
                self.wait("tensor", sv, v_R)
                self.wait("tensor", sv, v_N)
                for j in range(8):
                    self.mm(psU[:, j * 64:(j + 1) * 64], P_a[:, j * 128:(j + 1) * 128],
                            Rbuf[:, j * 64:(j + 1) * 64], start=True, stop=True,
                            sem=(sm if j == 7 else None))
                u_mm = sm.v
                self.wait("vector", sm, u_mm)
                v_U = self.v("tensor_tensor", Ubuf, Rbuf, psU, op=ALU.subtract, sem=sv)
                Pc, Pn, PTc, PTn = P_a, P_b, PT_a, PT_b
                self.wait("tensor", sv, v_A)
                for lev in range(6):
                    last = (lev == 5)
                    for j in range(8):
                        self.mm(PSA[:, j * 128:(j + 1) * 128],
                                PTc[:, j * 128:(j + 1) * 128],
                                Pc[:, j * 128:(j + 1) * 128], start=True, stop=True,
                                sem=(sm if (j == 7 and last) else None))
                    if not last:
                        for j in range(8):
                            self.mm(PSA[:, 1024 + j * 128:1024 + (j + 1) * 128],
                                    Pc[:, j * 128:(j + 1) * 128],
                                    PTc[:, j * 128:(j + 1) * 128], start=True, stop=True,
                                    sem=(sm if j == 7 else None))
                    sq_mm = sm.v
                    self.wait("scalar", sm, sq_mm)
                    s_p = self.scopy(Pn, PSA[:, 0:1024], sem=ss)
                    if not last:
                        self.wait("vector", sm, sq_mm)
                        v_pt = self.v("tensor_copy", PTn, PSA[:, 1024:2048], sem=sv)
                        self.wait("tensor", sv, v_pt)
                    self.wait("tensor", ss, s_p)
                    self.wait("tensor", sv, v_U)
                    for j in range(8):
                        self.mm(psU[:, j * 64:(j + 1) * 64], Pn[:, j * 128:(j + 1) * 128],
                                Ubuf[:, j * 64:(j + 1) * 64], start=True, stop=True,
                                sem=(sm if j == 7 else None))
                    ap_mm = sm.v
                    self.wait("vector", sm, ap_mm)
                    v_U = self.v("tensor_tensor", Ubuf, Ubuf, psU, op=ALU.add, sem=sv)
                    Pc, Pn = Pn, Pc
                    PTc, PTn = PTn, PTc
                if self.hg_stages < 4:
                    continue
                self.wait("tensor", sv, v_U)
                self.wait("tensor", sv, v_A)
                for j in range(8):
                    self.mm(PSB[0:64, j * 128:(j + 1) * 128],
                            self.S0b[:, hsl(j) * 64:(hsl(j) + 1) * 64],
                            qbar_w[:, j * 128:(j + 1) * 128], start=True, stop=False)
                    self.mm(PSB[0:64, j * 128:(j + 1) * 128],
                            Ubuf[:, j * 64:(j + 1) * 64],
                            Aqk[:, j * 128:(j + 1) * 128], start=False, stop=True,
                            sem=(sm if j == 7 else None))
                o_mm = sm.v
                self.wait("scalar", sm, o_mm)
                for j in range(8):
                    h = hsl(j)
                    self.scopy(yreg[64 * (h % 2):64 * (h % 2) + 64,
                                    (h // 2) * 128:(h // 2 + 1) * 128],
                               PSB[0:64, j * 128:(j + 1) * 128],
                               sem=(ss if j == 7 else None))
                if self.hg_stages < 5:
                    continue
                for j in range(8):
                    self.tr(PSA[:, j * 32:j * 32 + 32].bitcast(BF16),
                            khat_w[:, j * 128:(j + 1) * 128], self.identb[0:64, 0:64],
                            sem=(sm if j == 7 else None))
                kt_mm = sm.v
                self.wait("scalar", sm, kt_mm)
                s_kt = self.scopy(khT_w, PSA[:, 0:256].bitcast(BF16), sem=ss)
                self.wait("tensor", ss, s_kt)
                for j in range(8):
                    self.mm(PSA[0:64, 1024 + j * 64:1024 + (j + 1) * 64],
                            khT_w[:, j * 64:(j + 1) * 64],
                            Ubuf[:, j * 64:(j + 1) * 64], start=True, stop=True,
                            sem=(sm if j == 7 else None))
                s_mm2 = sm.v
                self.wait("vector", sm, s_mm2)
                self.wait("vector", sm, o_mm)
                for j in range(8):
                    h = hsl(j)
                    self.v("scalar_tensor_tensor", self.S0f[:, h * 64:(h + 1) * 64],
                           self.S0f[:, h * 64:(h + 1) * 64], gCc[:, j:j + 1],
                           PSA[0:64, 1024 + j * 64:1024 + (j + 1) * 64],
                           op0=ALU.mult, op1=ALU.add)
                self.v("tensor_copy", self.S0b[:, h0 * 64:(h0 + 7) * 64],
                       self.S0f[:, h0 * 64:(h0 + 7) * 64])
                self.v("tensor_copy", self.S0b[:, (h0 + 7) * 64:(h0 + 8) * 64],
                       self.S0f[:, (h0 + 7) * 64:(h0 + 8) * 64], sem=sv)

            if self.pa_stages < 4:
                continue
            self.bar()
            v_sq = self.v("tensor_mul", ysq, yreg, yreg, sem=sv)
            self.wait("tensor", sv, v_sq)
            for j in range(8):
                self.mm(PSB[0:2, j * 128:(j + 1) * 128], self.e2[:, :],
                        ysq[:, j * 128:(j + 1) * 128], start=True, stop=True,
                        sem=(sm if j == 7 else None))
            rms_mm = sm.v
            self.wait("vector", sm, rms_mm)
            self.v("tensor_scalar", rmsrow[0:2, 0:512], PSB[0:2, 0:512], 1.0 / 64.0,
                   L2_EPS, op0=ALU.mult, op1=ALU.add)
            self.v("tensor_scalar", rmsrow[0:2, 512:1024], PSB[0:2, 512:1024],
                   1.0 / 64.0, L2_EPS, op0=ALU.mult, op1=ALU.add)
            self.v("reciprocal", rmsrow[0:2, 0:512], rmsrow[0:2, 0:512])
            v_rr = self.v("reciprocal", rmsrow[0:2, 512:1024], rmsrow[0:2, 512:1024],
                          sem=sv)
            self.wait("scalar", sv, v_rr)
            s_rr = self.act(rmsrow, rmsrow, AF.Sqrt, sem=ss)
            self.wait("vector", ss, s_rr)
            v_rb = self.v("tensor_copy", rmsrow_b, rmsrow, sem=sv)
            self.wait("tensor", sv, v_rb)
            for j in range(8):
                self.mm(PSB[:, j * 128:(j + 1) * 128], self.sel2[:, :],
                        rmsrow_b[:, j * 128:(j + 1) * 128], start=True, stop=True,
                        sem=(sm if j == 7 else None))
            rep_mm = sm.v
            s_sz = self.act(zreg, zreg, AF.Silu, sem=ss)
            self.wait("vector", ss, s_sz)
            self.wait("vector", sm, rep_mm)
            self.v("tensor_mul", ysq[:, 0:512], yreg[:, 0:512], zreg[:, 0:512])
            self.v("tensor_mul", ysq[:, 512:1024], yreg[:, 512:1024], zreg[:, 512:1024])
            self.v("tensor_mul", ysq[:, 0:512], ysq[:, 0:512], PSB[:, 0:512])
            self.v("tensor_mul", ysq[:, 512:1024], ysq[:, 512:1024], PSB[:, 512:1024])
            self.v("tensor_scalar_mul", ysq[:, 0:512], ysq[:, 0:512], self.normw[:, 0:1])
            v_yg = self.v("tensor_scalar_mul", ysq[:, 512:1024], ysq[:, 512:1024],
                          self.normw[:, 0:1], sem=sv)
            self.wait("scalar", sv, v_yg)
            yg_store[c] = self.dma_fm_out("scalar", qs, self.d_yg[:, c0:c0 + 128], ysq, 8, 128)
            if self.debug:
                self.wait("sync", sv, v_yg)
                self.dma_fm_out("sync", qh, self.dbg["yg"][:, c0:c0 + 128], ysq, 8, 128)
        self.pa_last_store = qs.v

    def _phase_pa2(self):
        T = self.T
        qg = self.qwo
        sm, ss, sv = self.sm, self.ss, self.sv
        B, Dp = self.B, self.Dp
        PSA = self.PSA
        self.bar(queues=True)
        wo_sb = B[:, 0:8192]
        yg_sb = [B[:, 8192 + i * 4096:8192 + (i + 1) * 4096] for i in range(2)]
        ld_wo = 0
        for kt in range(8):
            ld_wo = self.dma("gpsimd", qg, wo_sb[:, kt * 1024:(kt + 1) * 1024],
                             self.d_W_o[kt * 128:(kt + 1) * 128, :])
        self.wait("sync", self.so_yg, self.pa_last_store)
        nsb = T // 512
        evs = []
        stores = []
        mm_done = {}
        lds = {}
        for i in range(min(2, nsb)):
            lds[i] = self.dma_fm("sync", self.qyg[i % 2], yg_sb[i % 2],
                                 self.d_yg[:, i * 512:(i + 1) * 512], 8, 512)
        for i in range(nsb):
            self.wait("tensor", self.qyg[i % 2], lds[i])
            if i == 0:
                self.wait("tensor", qg, ld_wo)
            for m in range(8):
                gi = i * 8 + m
                slot = PSA[:, (m % 4) * 512:(m % 4 + 1) * 512]
                if gi >= 4:
                    self.wait("tensor", ss, evs[gi - 4])
                mmv = None
                for kt in range(8):
                    mmv = self.mm(slot, wo_sb[:, kt * 1024 + m * 128:kt * 1024 + (m + 1) * 128],
                                  yg_sb[i % 2][:, kt * 512:(kt + 1) * 512],
                                  start=(kt == 0), stop=(kt == 7),
                                  sem=(sm if kt == 7 else None))
                self.wait("scalar", sm, mmv)
                stage = Dp[:, (gi % 4) * 512:(gi % 4 + 1) * 512]
                if len(stores) >= 2:
                    self.wait("scalar", self.so_s[gi % 2], stores[-2])
                ev = self.scopy(stage, slot, sem=ss)
                evs.append(ev)
                stores.append(self.dma("scalar", self.so_s[gi % 2],
                                       self.d_sS[m * 128:(m + 1) * 128,
                                                 i * 512:(i + 1) * 512], stage))
                if self.debug:
                    self.dma("scalar", self.sdbg, self.dbg["sT"][m * 128:(m + 1) * 128,
                                                                 i * 512:(i + 1) * 512], stage)
            mm_done[i] = sm.v
            if i + 2 < nsb:
                self.wait("sync", sm, mm_done[i])
                lds[i + 2] = self.dma_fm("sync", self.qyg[i % 2], yg_sb[i % 2],
                                         self.d_yg[:, (i + 2) * 512:(i + 3) * 512], 8, 512)
        self.pa2_store = (self.so_s[0].v, self.so_s[1].v)

    def _phase_pb1(self):
        T, NT = self.T, self.NT
        qg = self.qcwf
        sm, ss, sv = self.sm, self.ss, self.sv
        B, Cp = self.B, self.Cp
        PSA = self.PSA
        xnT = self.xnT
        self.bar(queues=True)
        cw_sb = B[:, 0:24576]
        ld = 0
        for tap in range(3):
            for kt in range(4):
                ld = self.dma_fm("gpsimd", qg,
                                 cw_sb[:, tap * 8192 + kt * 2048:tap * 8192 + (kt + 1) * 2048],
                                 self.d_cwf[tap, kt * 256:(kt + 1) * 256, :], 2, 1024)
        self.wait("tensor", qg, ld)
        evs = []
        stores = []
        for i in range(NT):
            c0 = i * 128
            for m in range(8):
                gi = i * 8 + m
                slot = PSA[:, (m % 4) * 512:(m % 4) * 512 + 128]
                if gi >= 4:
                    self.wait("tensor", ss, evs[gi - 4])
                plan = []
                for tap in (1, 0, 2):
                    lo = c0 + tap - 1
                    dlo = max(0, -lo)
                    dhi = 128 - max(0, lo + 128 - T)
                    if dhi <= dlo:
                        continue
                    for kt in range(8):
                        plan.append((tap, kt, dlo, dhi, lo))
                mmv = None
                for pi, (tap, kt, dlo, dhi, lo) in enumerate(plan):
                    mmv = self.mm(
                        slot[:, dlo:dhi],
                        cw_sb[:, tap * 8192 + kt * 1024 + m * 128:
                              tap * 8192 + kt * 1024 + (m + 1) * 128],
                        xnT[:, kt * T + lo + dlo:kt * T + lo + dhi],
                        start=(pi == 0), stop=(pi == len(plan) - 1),
                        sem=(sm if pi == len(plan) - 1 else None))
                self.wait("scalar", sm, mmv)
                stage = Cp[:, (gi % 8) * 128:(gi % 8 + 1) * 128]
                if len(stores) >= 2:
                    self.wait("scalar", self.so_xc[gi % 2], stores[-2])
                evs.append(self.act(stage, slot[:, 0:128], AF.Identity,
                                    bias=self.cbf[:, m:m + 1], sem=ss))
                stores.append(self.dma("scalar", self.so_xc[gi % 2],
                                       self.d_xcB[m * 128:(m + 1) * 128, c0:c0 + 128],
                                       stage))
                if self.debug:
                    self.dma("scalar", self.sdbg, self.dbg["xcT"][m * 128:(m + 1) * 128,
                                                                  c0:c0 + 128], stage)
        self.pb1_store = (self.so_xc[0].v, self.so_xc[1].v)

    def _phase_pb2a(self):
        T, NT, NQB = self.T, self.NT, self.NQB
        qg = self.qwqkv
        sm, ss, sv = self.sm, self.ss, self.sv
        A, B, Cp, Dp = self.A, self.B, self.Cp, self.Dp
        PSA, PSB = self.PSA, self.PSB
        self.bar(queues=True)
        self.wait("sync", self.so_xc[0], self.pb1_store[0])
        self.wait("sync", self.so_xc[1], self.pb1_store[1])
        kT = A
        vT = B
        vT_all = B[:, 0:NT * 1040]
        vmset = self.v("memset", vT_all.rearrange("p (a c) -> p a c", c=65)[:, :, 64:65],
                       1.0, sem=sv)
        self.wait("scalar", sv, vmset)
        wslc = Cp[:, 4096:12288]
        xcb = [Cp[:, 0:2048], Cp[:, 2048:4096]]
        qstage = [Dp[:, i * 128:(i + 1) * 128].bitcast(BF16) for i in range(2)]
        vstage = [Dp[:, 256 + i * 128:256 + (i + 1) * 128].bitcast(BF16) for i in range(2)]
        nch = T // 256
        for phase, wofs in (("k", 1024), ("v", 2048), ("q", 0)):
            self.wait("gpsimd", self.qwqkv, self.qwqkv.v)
            self.wait("gpsimd", self.sm, self.sm.v)
            self.wait("sync", self.sm, self.sm.v)
            ldw = 0
            for kt in range(8):
                ldw = self.dma("gpsimd", qg, wslc[:, kt * 1024:(kt + 1) * 1024],
                               self.d_wqkvT[kt * 128:(kt + 1) * 128, wofs:wofs + 1024])
            lds = {}
            for ci in range(min(2, nch)):
                lds[ci] = self.dma_fm("sync", self.qxc[ci % 2], xcb[ci % 2],
                                      self.d_xcB[:, ci * 256:(ci + 1) * 256], 8, 256)
            evq = []
            trq = []
            qstores = []
            mmd = {}
            for ci in range(nch):
                self.wait("tensor", self.qxc[ci % 2], lds[ci])
                if ci == 0:
                    self.wait("tensor", qg, ldw)
                for m in range(8):
                    gi = ci * 8 + m
                    slot = PSB[:, 1024 + (m % 2) * 512:1024 + (m % 2) * 512 + 256]
                    if gi >= 2:
                        self.wait("tensor", ss, evq[gi - 2])
                    mmv = None
                    for kt in range(8):
                        mmv = self.mm(slot, wslc[:, kt * 1024 + m * 128:kt * 1024 + (m + 1) * 128],
                                      xcb[ci % 2][:, kt * 256:(kt + 1) * 256],
                                      start=(kt == 0), stop=(kt == 7),
                                      sem=(sm if kt == 7 else None))
                    self.wait("scalar", sm, mmv)
                    if phase == "k":
                        evq.append(self.act(kT[:, m * T + ci * 256:m * T + (ci + 1) * 256],
                                            slot, AF.Identity, bias=self.bqkv[:, 8 + m:9 + m],
                                            sem=ss))
                    elif phase == "q":
                        stq = qstage[m % 2]
                        if len(qstores) >= 2:
                            self.wait("scalar", self.so_q[gi % 2], qstores[-2])
                        ev = self.act(stq, slot, AF.Identity, bias=self.bqkv[:, m:m + 1],
                                      sem=ss)
                        evq.append(ev)
                        qstores.append(self.dma("scalar", self.so_q[gi % 2],
                                                self.d_qB[m * 128:(m + 1) * 128,
                                                          ci * 256:(ci + 1) * 256], stq))
                    else:
                        stv = vstage[m % 2]
                        ev = self.act(stv, slot, AF.Identity, bias=self.bqkv[:, 16 + m:17 + m],
                                      sem=ss)
                        evq.append(ev)
                        self.wait("tensor", ss, ev)
                        ev2 = None
                        for sub in range(2):
                            tt = ci * 2 + sub
                            ps = PSA[:, (tt % 4) * 512:(tt % 4) * 512 + 64].bitcast(BF16)
                            if len(trq) >= 4:
                                self.wait("tensor", ss, trq[-4])
                            tv = self.tr(ps, stv[:, sub * 128:(sub + 1) * 128],
                                         self.identb[:, :], sem=sm)
                            self.wait("scalar", sm, tv)
                            dst = vT[:, tt * 1040 + (2 * m) * 65:tt * 1040 + (2 * m) * 65 + 130]
                            dst3 = dst.rearrange("p (a c) -> p a c", c=65)[:, :, 0:64]
                            ev2 = self.scopy(dst3, ps.rearrange("p (a c) -> p a c", c=64),
                                             sem=ss)
                            trq.append(ev2)
                mmd[ci] = sm.v
                if ci + 2 < nch:
                    self.wait("sync", sm, mmd[ci])
                    lds[ci + 2] = self.dma_fm("sync", self.qxc[ci % 2], xcb[ci % 2],
                                              self.d_xcB[:, (ci + 2) * 256:(ci + 3) * 256],
                                              8, 256)
            self.bar()
        qsb = Cp[:, 0:4096]
        stage = [[A[:, 32768 + j * 512:32768 + (j + 1) * 512] for j in range(8)],
                 [Cp[:, 8192 + j * 512:8192 + (j + 1) * 512] for j in range(8)]]
        ctxb_e = [Cp[:, 4096 + i * 512:4096 + (i + 1) * 512] for i in range(2)]
        ctxb_o = [Cp[:, 5120 + i * 512:5120 + (i + 1) * 512] for i in range(2)]
        sums2 = [Dp[0:1, 0:512], Dp[0:1, 1024:1536]]
        rsum2 = [Dp[0:1, 512:1024], Dp[0:1, 1536:2048]]
        rsum2b = [Dp[0:1, 2048:2304].bitcast(BF16), Dp[0:1, 2304:2560].bitcast(BF16)]
        rep2 = [Dp[0:64, 2560:2816].bitcast(BF16), Dp[0:64, 2816:3072].bitcast(BF16)]
        psPV2 = [PSB[:, 0:512], PSB[:, 512:1024]]
        NB = NT // 4
        self.wait("sync", self.so_q[0], self.so_q[0].v)
        self.wait("sync", self.so_q[1], self.so_q[1].v)
        ld_q = {0: self.dma_fm("sync", self.qq, qsb, self.d_qB[:, 0:QB], 8, QB)}
        ctx_stores = []
        prev_exp = None
        prev_pv = None
        v_ctx_prev = 0
        for qb in range(NQB):
            self.wait("tensor", self.qq, ld_q[qb])
            last_sc_mm = 0
            for pr in range(8):
                exp_pair = [[], []]
                pv_pair = [[], []]
                sc_pair = [[], []]
                for b in range(NB + 1):
                    if b < NB:
                        for kt in range(4 * b, 4 * b + 4):
                            for p in (0, 1):
                                if kt >= 2:
                                    self.wait("tensor", ss, exp_pair[p][kt - 2])
                                elif prev_exp is not None:
                                    self.wait("tensor", ss,
                                              prev_exp[p][NT - 2 + kt])
                                scv = self.mm(
                                    PSA[:, (p * 2 + kt % 2) * 512:
                                        (p * 2 + kt % 2 + 1) * 512],
                                    kT[64 * p:64 * p + 64,
                                       pr * T + kt * 128:pr * T + (kt + 1) * 128],
                                    qsb[64 * p:64 * p + 64, pr * 512:(pr + 1) * 512],
                                    start=True, stop=True, sem=sm)
                                sc_pair[p].append(scv)
                                self.wait("scalar", sm, scv)
                                if kt >= 8:
                                    self.wait("scalar", sm, pv_pair[p][kt - 8])
                                elif prev_pv is not None:
                                    self.wait("scalar", sm,
                                              prev_pv[p][NT - 8 + kt])
                                pe = self.act(stage[p][kt % 8],
                                              PSA[:, (p * 2 + kt % 2) * 512:
                                                  (p * 2 + kt % 2 + 1) * 512],
                                              AF.Exp, bias=self.kc[:, 2:3],
                                              scale=0.125, sem=ss)
                                exp_pair[p].append(pe)
                    if b >= 1:
                        for kt in range(4 * (b - 1), 4 * (b - 1) + 4):
                            for p in (0, 1):
                                self.wait("tensor", ss, exp_pair[p][kt])
                                if kt == 0:
                                    self.wait("tensor", sv, v_ctx_prev)
                                self.mm(psPV2[p][0:65, :],
                                        vT[:, kt * 1040 + (2 * pr + p) * 65:
                                           kt * 1040 + (2 * pr + p + 1) * 65],
                                        stage[p][kt % 8],
                                        start=(kt == 0), stop=(kt == NT - 1),
                                        sem=sm)
                                pv_pair[p].append(sm.v)
                prev_exp = exp_pair
                prev_pv = pv_pair
                last_sc_mm = max(last_sc_mm, sc_pair[0][-1], sc_pair[1][-1])
                pv_last = sm.v
                self.wait("scalar", sm, pv_last)
                self.scopy(sums2[0], psPV2[0][64:65, :])
                s_sum = self.scopy(sums2[1], psPV2[1][64:65, :], sem=ss)
                self.wait("vector", ss, s_sum)
                self.v("reciprocal", rsum2[0], sums2[0])
                self.v("reciprocal", rsum2[1], sums2[1])
                self.v("tensor_copy", rsum2b[0], rsum2[0])
                v_rs = self.v("tensor_copy", rsum2b[1], rsum2[1], sem=sv)
                self.wait("tensor", sv, v_rs)
                self.mm(PSB[0:64, 1024:1536], self.sel2[0:1, 0:64],
                        rsum2b[0], start=True, stop=True)
                rpmm = self.mm(PSB[0:64, 1536:2048], self.sel2[0:1, 0:64],
                               rsum2b[1], start=True, stop=True, sem=sm)
                self.wait("scalar", sm, rpmm)
                self.scopy(rep2[0], PSB[0:64, 1024:1536])
                s_rep = self.scopy(rep2[1], PSB[0:64, 1536:2048], sem=ss)
                self.wait("vector", ss, s_rep)
                if len(ctx_stores) >= 2:
                    self.wait("vector", self.so_ctx[pr % 2], ctx_stores[-2][1])
                self.v("tensor_mul", ctxb_e[pr % 2][0:64, :], psPV2[0][0:64, :],
                       rep2[0])
                v_ctx = self.v("tensor_mul", ctxb_o[pr % 2][0:64, :],
                               psPV2[1][0:64, :], rep2[1], sem=sv)
                v_ctx_prev = v_ctx
                self.wait("scalar", sv, v_ctx)
                st_e = self.dma("scalar", self.so_ctx[pr % 2],
                                self.d_ctxB[(2 * pr) * 64:(2 * pr + 1) * 64,
                                            qb * QB:(qb + 1) * QB],
                                ctxb_e[pr % 2][0:64, :])
                st_o = self.dma("scalar", self.so_ctx[pr % 2],
                                self.d_ctxB[(2 * pr + 1) * 64:(2 * pr + 2) * 64,
                                            qb * QB:(qb + 1) * QB],
                                ctxb_o[pr % 2][0:64, :])
                ctx_stores.append((st_e, st_o))
            if qb + 1 < NQB:
                self.wait("sync", sm, last_sc_mm)
                ld_q[qb + 1] = self.dma_fm("sync", self.qq, qsb,
                                           self.d_qB[:, (qb + 1) * QB:(qb + 2) * QB], 8, QB)
        self.pb2a_store = (self.so_ctx[0].v, self.so_ctx[1].v)
        if self.debug:
            self.wait("sync", self.so_ctx[0], self.pb2a_store[0])
            self.wait("sync", self.so_ctx[1], self.pb2a_store[1])
            for ft in range(8):
                self.dma("sync", self.sdbg, self.dbg["ctxT"][ft * 128:(ft + 1) * 128, :],
                         self.d_ctxB[ft * 128:(ft + 1) * 128, :])

    def _phase_pb2b(self):
        T, NQB = self.T, self.NQB
        qg = self.qwo
        sm, ss, sv = self.sm, self.ss, self.sv
        B, Cp, Dp = self.B, self.Cp, self.Dp
        PSA, PSB = self.PSA, self.PSB
        self.bar(queues=True)
        self.wait("sync", self.so_ctx[0], self.pb2a_store[0])
        self.wait("sync", self.so_ctx[1], self.pb2a_store[1])
        self.wait("sync", self.qxc[0], self.qxc[0].v)
        self.wait("sync", self.qxc[1], self.qxc[1].v)
        self.wait("gpsimd", self.qwo, self.qwo.v)
        wo_sb = B[:, 0:8192]
        ldw = 0
        for kt in range(8):
            ldw = self.dma("gpsimd", qg, wo_sb[:, kt * 1024:(kt + 1) * 1024],
                           self.d_woT[kt * 128:(kt + 1) * 128, :])
        ctxb = Cp[:, 0:4096]
        xcb = Cp[:, 4096:8192]
        r1 = Cp[:, 8192:12288]
        sqst = [B[:, 8192 + i * 512:8192 + (i + 1) * 512] for i in range(2)]
        eT = B[0:8, 9216:9728]
        x1st = [B[:, 16384 + i * 1024:16384 + (i + 1) * 1024].bitcast(F32) for i in range(2)]
        x1bst = [B[:, 20480 + i * 512:20480 + (i + 1) * 512] for i in range(2)]
        UW = Dp[:, 0:1024]
        uwb = Dp[0:1, 1024:1536].bitcast(BF16)
        WSt = [Dp[:, 1536 + i * 512:1536 + (i + 1) * 512] for i in range(2)]
        statv = B[0:1, 22528:25600].bitcast(F32)
        mean = statv[0:1, 0:512]
        var = statv[0:1, 512:1024]
        tmpr = statv[0:1, 1024:1536]
        ldc = {0: (self.dma_fm("sync", self.qctx, ctxb, self.d_ctxB[:, 0:QB], 8, QB),
                   self.dma_fm("sync", self.qxc[0], xcb, self.d_xcB[:, 0:QB], 8, QB))}
        x1_stores = []
        for qb in range(NQB):
            c0 = qb * QB
            self.wait("tensor", self.qctx, ldc[qb][0])
            self.wait("tensor", self.qxc[0], ldc[qb][1])
            if qb == 0:
                self.wait("tensor", qg, ldw)
            psSum = PSB[0:1, 1024:1536]
            psSsq = PSB[0:1, 1536:2048]
            stat_mms = []
            for m in range(8):
                slot = PSA[:, (m % 2) * 512:(m % 2) * 512 + 512]
                mmv = None
                for kt in range(8):
                    mmv = self.mm(slot, wo_sb[:, kt * 1024 + m * 128:kt * 1024 + (m + 1) * 128],
                                  ctxb[:, kt * 512:(kt + 1) * 512],
                                  start=(kt == 0), stop=(kt == 7),
                                  sem=(sm if kt == 7 else None))
                self.wait("scalar", sm, mmv)
                s_t = self.act(WSt[m % 2], slot, AF.Identity, bias=self.bo[:, m:m + 1], sem=ss)
                self.wait("vector", ss, s_t)
                r1s = r1[:, m * 512:(m + 1) * 512]
                self.v("tensor_add", r1s[:, 0:256], WSt[m % 2][:, 0:256],
                       xcb[:, m * 512:m * 512 + 256])
                self.v("tensor_add", r1s[:, 256:512], WSt[m % 2][:, 256:512],
                       xcb[:, m * 512 + 256:(m + 1) * 512])
                if len(stat_mms) >= 2:
                    self.wait("vector", sm, stat_mms[-2])
                self.v("tensor_mul", sqst[m % 2][:, 0:256], r1s[:, 0:256], r1s[:, 0:256])
                v_sq = self.v("tensor_mul", sqst[m % 2][:, 256:512],
                              r1s[:, 256:512], r1s[:, 256:512], sem=sv)
                self.wait("tensor", sv, v_sq)
                self.mm(psSum, self.onesb[:, 0:1], r1[:, m * 512:(m + 1) * 512],
                        start=(m == 0), stop=(m == 7), sem=None)
                stat_mms.append(self.mm(psSsq, self.onesb[:, 0:1], sqst[m % 2],
                                        start=(m == 0), stop=(m == 7), sem=sm))
            st_mm = sm.v
            self.wait("vector", sm, st_mm)
            self.v("tensor_scalar_mul", mean, psSum, 1.0 / 1024.0)
            self.v("tensor_scalar_mul", var, psSsq, 1.0 / 1024.0)
            self.v("tensor_mul", tmpr, mean, mean)
            self.vsp_op()
            v_var = self.v("tensor_sub", var, var, tmpr, sem=sv)
            self.wait("scalar", sv, v_var)
            self.act(var, var, AF.Ln, bias=self.kc[0:1, 0:1])
            s_rstd = self.act(var, var, AF.Exp, scale=-0.5, sem=ss)
            self.wait("vector", ss, s_rstd)
            self.v("scalar_tensor_tensor", mean, mean, self.kc[0:1, 7:8], var,
                   op0=ALU.mult, op1=ALU.mult)
            self.v("tensor_copy", uwb[0:1, 0:512], var)
            v_uw = self.v("tensor_copy", uwb[0:1, 512:1024], mean, sem=sv)
            self.wait("tensor", sv, v_uw)
            bmm = None
            for half in range(2):
                bmm = self.mm(PSB[:, 1024 + half * 512:1536 + half * 512],
                              self.onesb[0:1, :], uwb[0:1, half * 512:(half + 1) * 512],
                              start=True, stop=True, sem=(sm if half == 1 else None))
            self.wait("vector", sm, bmm)
            v_uwr = self.v("tensor_copy", UW, PSB[:, 1024:2048], sem=sv)
            psG = PSB[0:8, 512:1024]
            g_mm = None
            gate_mms = []
            for m in range(8):
                self.wait("vector", sv, v_uwr)
                if len(x1_stores) >= 2:
                    self.wait("vector", self.so_x1[m % 2], x1_stores[-2][1])
                xs = x1st[m % 2]
                xb = x1bst[m % 2]
                r1s = r1[:, m * 512:(m + 1) * 512]
                self.v("tensor_mul", xs[:, 0:256], r1s[:, 0:256], UW[:, 0:256])
                self.v("tensor_mul", xs[:, 256:512], r1s[:, 256:512], UW[:, 256:512])
                self.v("tensor_add", xs[:, 0:256], xs[:, 0:256], UW[:, 512:768])
                v_t = self.v("tensor_add", xs[:, 256:512], xs[:, 256:512],
                             UW[:, 768:1024], sem=sv)
                self.wait("scalar", sv, v_t)
                s_x1 = self.act(xs[:, 0:512], xs[:, 0:512], AF.Identity,
                                bias=self.ln1b[:, m:m + 1], scale=self.ln1g[:, m:m + 1],
                                sem=ss)
                self.wait("vector", ss, s_x1)
                if len(gate_mms) >= 2:
                    self.wait("vector", sm, gate_mms[-2])
                v_xb = self.v("tensor_copy", xb, xs[:, 0:512], sem=sv)
                self.wait("scalar", sv, v_xb)
                st1 = self.dma("scalar", self.so_x1[m % 2],
                               self.d_x1F[m * 128:(m + 1) * 128, c0:c0 + QB], xs[:, 0:512])
                st2 = self.dma("scalar", self.so_x1[m % 2],
                               self.d_x1B[m * 128:(m + 1) * 128, c0:c0 + QB], xb)
                if self.debug:
                    self.dma("scalar", self.sdbg, self.dbg["x1T"][m * 128:(m + 1) * 128,
                                                                  c0:c0 + QB], xs[:, 0:512])
                x1_stores.append((st1, st2))
                self.wait("tensor", sv, v_xb)
                g_mm = self.mm(psG, self.gatew[:, m * 8:(m + 1) * 8], xb,
                               start=(m == 0), stop=(m == 7), sem=sm)
                gate_mms.append(g_mm)
            self.wait("scalar", sm, g_mm)
            s_e = self.act(eT, psG, AF.Exp, bias=self.gateb[:, 0:1], sem=ss)
            self.wait("tensor", ss, s_e)
            smm = self.mm(PSB[0:1, 0:512], self.onesb[0:8, 0:1], eT,
                          start=True, stop=True, sem=sm)
            self.wait("vector", sm, smm)
            prb = Dp[0:1, 2560 + 256:2560 + 512].bitcast(BF16)
            self.v("reciprocal", tmpr[0:1, 0:256], PSB[0:1, 0:256])
            self.v("reciprocal", tmpr[0:1, 256:512], PSB[0:1, 256:512])
            self.v("tensor_copy", prb[0:1, 0:256], tmpr[0:1, 0:256])
            v_pr = self.v("tensor_copy", prb[0:1, 256:512], tmpr[0:1, 256:512], sem=sv)
            self.wait("tensor", sv, v_pr)
            pmm = self.mm(PSB[0:8, 0:512], self.onesb[0:1, 0:8],
                          prb, start=True, stop=True, sem=sm)
            self.wait("vector", sm, pmm)
            v_probs = self.v("tensor_mul", self.probsT[:, c0:c0 + QB], eT,
                             PSB[0:8, 0:512], sem=sv)
            if self.debug:
                self.wait("sync", sv, v_probs)
                self.dma("sync", self.sdbg, self.dbg["probs"][:, c0:c0 + QB],
                         self.probsT[:, c0:c0 + QB])
            if qb + 1 < NQB:
                self.wait("sync", sm, st_mm)
                ldc[qb + 1] = (self.dma_fm("sync", self.qctx, ctxb,
                                           self.d_ctxB[:, (qb + 1) * QB:(qb + 2) * QB], 8, QB),
                               self.dma_fm("sync", self.qxc[0], xcb,
                                           self.d_xcB[:, (qb + 1) * QB:(qb + 2) * QB], 8, QB))
        self.pb2b_store = (self.so_x1[0].v, self.so_x1[1].v)

    def _phase_pc(self):
        T, NQB = self.T, self.NQB
        qh, qg = self.qx1b, self.qw1
        sm, ss, sv = self.sm, self.ss, self.sv
        A, B, Cp, Dp = self.A, self.B, self.Cp, self.Dp
        PSA, PSB = self.PSA, self.PSB
        TH = T // 2
        NTB = TH // 512
        moe = A[:, 0:32768].bitcast(F32)
        x1b = B[:, 0:16384]
        prep = B[:, 16384:16896]
        h_sb = B[:, 16896:33280]
        w1buf = [Cp[:, i * 1024:(i + 1) * 1024] for i in range(8)]
        w2buf = [Cp[:, 8192 + i * 512:8192 + (i + 1) * 512] for i in range(8)]
        self.bar(queues=True)
        self.wait("sync", self.so_x1[0], self.pb2b_store[0])
        self.wait("sync", self.so_x1[1], self.pb2b_store[1])
        self.wait("sync", self.qx[0], self.qx[0].v)
        self.wait("sync", self.qx[1], self.qx[1].v)
        self.wait("sync", self.qb1, self.qb1.v)
        self.wait("gpsimd", self.qwin, self.qwin.v)
        self.dma("sync", self.qb1, self.sel8b[:, :], self.d_sel8b[:, :])
        self.wait("tensor", self.qb1, self.qb1.v)
        self._pc_w1next = None
        self._pc_addh = []
        ld_pc = self.dma("gpsimd", self.qwin, self.b2sb[:, :], self.d_b2[:, :])
        self.wait("tensor", self.qwin, ld_pc)
        for hf in range(2):
            hof = hf * TH
            self.bar(queues=True)
            ldx = 0
            for ft in range(4):
                ldx = self.dma_fm("sync", qh, x1b[:, ft * 2 * TH:(ft + 1) * 2 * TH],
                                  self.d_x1B[ft * 256:(ft + 1) * 256, hof:hof + TH],
                                  2, TH)
            self.wait("tensor", qh, ldx)
            binit_ev = []
            for m in range(8):
                for tb in range(NTB):
                    gi = m * NTB + tb
                    slot = PSA[:, (gi % 4) * 512:(gi % 4 + 1) * 512]
                    if gi >= 4:
                        self.wait("tensor", ss, binit_ev[gi - 4])
                    bm = self.mm(slot, self.b2sb[:, m * 128:(m + 1) * 128],
                                 self.probsT[:, hof + tb * 512:hof + (tb + 1) * 512],
                                 start=True, stop=True, sem=sm)
                    self.wait("scalar", sm, bm)
                    binit_ev.append(self.scopy(
                        moe[:, m * TH + tb * 512:m * TH + (tb + 1) * 512], slot, sem=ss))
            moe_init_s = ss.v
            self.wait("vector", ss, moe_init_s)
            self.wait("tensor", ss, moe_init_s)
            for e in range(NE):
                self.wait("sync", ss, ss.v)
                self.wait("sync", self.qb1, self.qb1.v)
                ldb1 = self.dma("sync", self.qb1, self.b1sb[:, (e % 2) * 32:(e % 2) * 32 + 32],
                                self.d_b1[e, :, :])
                for jb in range(4):
                    if self._pc_w1next is not None:
                        ldw1 = self._pc_w1next
                        self._pc_w1next = None
                    else:
                        ldw1 = {}
                        self.wait("gpsimd", sm, sm.v)
                        for j8 in range(8):
                            j = jb * 8 + j8
                            ldw1[j8] = self.dma("gpsimd", self.qw1, w1buf[j8],
                                                self.d_w1[e, j, :, :, :].rearrange(
                                                    "p a c -> p (a c)"))
                    if self._pc_addh:
                        self.wait("tensor", sv, self._pc_addh[-1])
                    self.wait("gpsimd", sm, sm.v)
                    ldw2_0 = {}
                    for j8 in range(8):
                        ldw2_0[j8] = self.dma("gpsimd", self.qw2, w2buf[j8],
                                              self.d_w2[e, jb * 8 + j8, 0, :, :])
                    h_evs = []
                    fold_last = 0
                    for tb in range(NTB):
                        t0g = hof + tb * 512
                        prmm = self.mm(PSB[:, 1024:1536],
                                       self.sel8b[:, e * 128:(e + 1) * 128],
                                       self.probsT[:, t0g:t0g + 512],
                                       start=True, stop=True, sem=sm)
                        self.wait("scalar", sm, prmm)
                        self.wait("scalar", sv, fold_last)
                        s_pr = self.scopy(prep[:, 0:512], PSB[:, 1024:1536], sem=ss)
                        self.wait("scalar", self.qb1, ldb1)
                        if tb == 0:
                            self.wait("tensor", self.qw1, ldw1[7])
                        for j8 in range(8):
                            j = jb * 8 + j8
                            slot = PSA[:, (j8 % 2) * 512:(j8 % 2) * 512 + 512]
                            if len(h_evs) >= 2:
                                self.wait("tensor", ss, h_evs[-2][0])
                            mmv = None
                            for kt in range(8):
                                mmv = self.mm(
                                    slot, w1buf[j8][:, kt * 128:(kt + 1) * 128],
                                    x1b[:, kt * TH + tb * 512:kt * TH + (tb + 1) * 512],
                                    start=(kt == 0), stop=(kt == 7),
                                    sem=(sm if kt == 7 else None))
                            self.wait("scalar", sm, mmv)
                            hv = self.act(h_sb[:, (tb * 8 + j8) * 512:(tb * 8 + j8 + 1) * 512],
                                          slot, AF.Relu,
                                          bias=self.b1sb[:, (e % 2) * 32 + j:
                                                         (e % 2) * 32 + j + 1], sem=ss)
                            self.wait("vector", ss, hv)
                            self.wait("vector", ss, s_pr)
                            fv = self.v("tensor_mul",
                                        h_sb[:, (tb * 8 + j8) * 512:(tb * 8 + j8 + 1) * 512],
                                        h_sb[:, (tb * 8 + j8) * 512:(tb * 8 + j8 + 1) * 512],
                                        prep[:, 0:512], sem=sv)
                            h_evs.append((hv, fv))
                            fold_last = fv
                    h_done = sv.v
                    hmm_last = sm.v
                    for dh in range(2):
                        if dh == 0:
                            ldw2 = ldw2_0
                        else:
                            ldw2 = {}
                            self.wait("gpsimd", sm, sm.v)
                            for j8 in range(8):
                                j = jb * 8 + j8
                                ldw2[j8] = self.dma("gpsimd", self.qw2, w2buf[j8],
                                                    self.d_w2[e, j, dh, :, :])
                        if dh == 1:
                            ne_, njb = (e, jb + 1) if jb + 1 < 4 else (e + 1, 0)
                            if ne_ >= NE:
                                ne_ = None
                            if ne_ is not None:
                                nxt = {}
                                for j8 in range(8):
                                    nxt[j8] = self.dma(
                                        "gpsimd", self.qw1, w1buf[j8],
                                        self.d_w1[ne_, njb * 8 + j8, :, :, :]
                                        .rearrange("p a c -> p (a c)"))
                                self._pc_w1next = nxt
                        psYs = [[PSA[:, 1024:1536], PSA[:, 1536:2048],
                                 PSB[:, 0:512], PSB[:, 512:1024]],
                                [PSA[:, 0:512], PSA[:, 512:1024],
                                 PSB[:, 1024:1536], PSB[:, 1536:2048]]]
                        for tb in range(NTB):
                            gt = dh * NTB + tb
                            psY = psYs[gt % 2]
                            self.wait("tensor", sv, h_done)
                            mmy = None
                            if tb == 0:
                                self.wait("tensor", self.qw2, ldw2[7])
                            if len(self._pc_addh) >= 2:
                                self.wait("tensor", sv, self._pc_addh[-2])
                            for m in range(4):
                                for j8 in range(8):
                                    mmy = self.mm(
                                        psY[m],
                                        w2buf[j8][:, m * 128:(m + 1) * 128],
                                        h_sb[:, (tb * 8 + j8) * 512:(tb * 8 + j8 + 1) * 512],
                                        start=(j8 == 0), stop=(j8 == 7),
                                        sem=(sm if j8 == 7 else None))
                            ymm = sm.v
                            self.wait("vector", sm, ymm)
                            vlast = None
                            for m in range(4):
                                mo = dh * 4 + m
                                vlast = self.v(
                                    "tensor_add",
                                    moe[:, mo * TH + tb * 512:mo * TH + (tb + 1) * 512],
                                    moe[:, mo * TH + tb * 512:mo * TH + (tb + 1) * 512],
                                    psY[m],
                                    sem=(sv if m == 3 else None))
                            self._pc_addh.append(vlast)
            self.bar(queues=True)
            r2b = Cp[:, 0:4096]
            x1f = [Cp[:, 4096 + i * 1024:4096 + (i + 1) * 1024].bitcast(F32) for i in range(2)]
            sqs2 = [Cp[:, 6144 + i * 512:6144 + (i + 1) * 512] for i in range(2)]
            gst = [Cp[:, 7168 + i * 1024:7168 + (i + 1) * 1024].bitcast(F32) for i in range(2)]
            sst = [Cp[:, 9216 + i * 1024:9216 + (i + 1) * 1024].bitcast(F32) for i in range(2)]
            xst = [B[:, 4096 + i * 2048:4096 + (i + 1) * 2048].bitcast(F32) for i in range(2)]
            ost = [Dp[:, 1024 + i * 1024:1024 + (i + 1) * 1024] for i in range(2)]
            statv2 = B[0:1, 8192:11264].bitcast(F32)
            mean = statv2[0:1, 0:512]
            var = statv2[0:1, 512:1024]
            uwb2 = Dp[0:1, 3072:3584].bitcast(BF16)
            UW2 = B[:, 0:4096].bitcast(F32)
            out_stores = []
            sg_done = {}
            for qb in range(NTB):
                c0g = hof + qb * 512
                psSum = PSB[0:1, 1024:1536]
                psSsq = PSB[0:1, 1536:2048]
                ld_f = {}
                for m in range(8):
                    ld_f[m] = (self.dma("sync", self.qx1f[m % 2], x1f[m % 2][:, 0:512],
                                        self.d_x1F[m * 128:(m + 1) * 128, c0g:c0g + 512])
                               if m < 2 else None)
                stats_mm = None
                pcs_mms = []
                self.wait("tensor", sv, getattr(self, "pc_last_vo", 0))
                for m in range(8):
                    if ld_f[m] is None:
                        ld_f[m] = self.dma("sync", self.qx1f[m % 2], x1f[m % 2][:, 0:512],
                                           self.d_x1F[m * 128:(m + 1) * 128, c0g:c0g + 512])
                    self.wait("vector", self.qx1f[m % 2], ld_f[m])
                    r2s = r2b[:, m * 512:(m + 1) * 512]
                    self.v("tensor_add", r2s[:, 0:256], x1f[m % 2][:, 0:256],
                           moe[:, m * TH + qb * 512:m * TH + qb * 512 + 256])
                    self.v("tensor_add", r2s[:, 256:512], x1f[m % 2][:, 256:512],
                           moe[:, m * TH + qb * 512 + 256:m * TH + (qb + 1) * 512])
                    if len(pcs_mms) >= 2:
                        self.wait("vector", sm, pcs_mms[-2])
                    self.v("tensor_mul", sqs2[m % 2][:, 0:256], r2s[:, 0:256],
                           r2s[:, 0:256])
                    v_sq = self.v("tensor_mul", sqs2[m % 2][:, 256:512],
                                  r2s[:, 256:512], r2s[:, 256:512], sem=sv)
                    self.wait("tensor", sv, v_sq)
                    self.mm(psSum, self.onesb[:, 0:1], r2b[:, m * 512:(m + 1) * 512],
                            start=(m == 0), stop=(m == 7), sem=None)
                    stats_mm = self.mm(psSsq, self.onesb[:, 0:1], sqs2[m % 2],
                                       start=(m == 0), stop=(m == 7), sem=sm)
                    pcs_mms.append(stats_mm)
                    if m + 2 < 8:
                        self.wait("sync", sv, v_sq)
                        ld_f[m + 2] = self.dma("sync", self.qx1f[m % 2], x1f[m % 2][:, 0:512],
                                               self.d_x1F[(m + 2) * 128:(m + 3) * 128,
                                                          c0g:c0g + 512])
                self.wait("vector", sm, stats_mm)
                self.v("tensor_scalar_mul", mean, psSum, 1.0 / 1024.0)
                self.v("tensor_scalar_mul", var, psSsq, 1.0 / 1024.0)
                self.v("tensor_mul", statv2[0:1, 1024:1536], mean, mean)
                self.vsp_op()
                v_var = self.v("tensor_sub", var, var, statv2[0:1, 1024:1536], sem=sv)
                self.wait("scalar", sv, v_var)
                self.act(var, var, AF.Ln, bias=self.kc[0:1, 0:1])
                s_rstd = self.act(var, var, AF.Exp, scale=-0.5, sem=ss)
                self.wait("vector", ss, s_rstd)
                self.v("scalar_tensor_tensor", mean, mean, self.kc[0:1, 7:8], var,
                       op0=ALU.mult, op1=ALU.mult)
                self.v("tensor_copy", uwb2[0:1, 0:512], var)
                v_uw = self.v("tensor_copy", uwb2[0:1, 512:1024], mean, sem=sv)
                self.wait("tensor", sv, v_uw)
                bmm = None
                for half in range(2):
                    bmm = self.mm(PSB[:, 1024 + half * 512:1536 + half * 512],
                                  self.onesb[0:1, :], uwb2[0:1, half * 512:(half + 1) * 512],
                                  start=True, stop=True, sem=(sm if half == 1 else None))
                self.wait("vector", sm, bmm)
                v_uwr = self.v("tensor_copy", UW2[:, 0:1024], PSB[:, 1024:2048], sem=sv)
                ld_s = {}
                for m in range(2):
                    ld_s[m] = self.dma("sync", self.qss[m % 2], sst[m % 2][:, 0:512],
                                       self.d_sS[m * 128:(m + 1) * 128, c0g:c0g + 512])
                tr_mm = None
                tr_hist = []
                for m in range(8):
                    if m not in ld_s:
                        ld_s[m] = self.dma("sync", self.qss[m % 2], sst[m % 2][:, 0:512],
                                           self.d_sS[m * 128:(m + 1) * 128, c0g:c0g + 512])
                    self.wait("vector", sv, v_uwr)
                    gs_ = gst[m % 2]
                    if len(tr_hist) >= 2:
                        self.wait("vector", sm, tr_hist[-2])
                    r2s = r2b[:, m * 512:(m + 1) * 512]
                    self.v("tensor_mul", gs_[:, 0:256], r2s[:, 0:256], UW2[:, 0:256])
                    self.v("tensor_mul", gs_[:, 256:512], r2s[:, 256:512], UW2[:, 256:512])
                    self.v("tensor_add", gs_[:, 0:256], gs_[:, 0:256], UW2[:, 512:768])
                    v_t = self.v("tensor_add", gs_[:, 256:512], gs_[:, 256:512],
                                 UW2[:, 768:1024], sem=sv)
                    self.wait("scalar", sv, v_t)
                    s_g = self.act(gs_[:, 0:512], gs_[:, 0:512], AF.Silu,
                                   bias=self.ln2b[:, m:m + 1], scale=self.ln2g[:, m:m + 1],
                                   sem=ss)
                    self.wait("vector", ss, s_g)
                    self.wait("vector", self.qss[m % 2], ld_s[m])
                    v_p = self.v("tensor_mul", gs_[:, 0:512], gs_[:, 0:512],
                                 sst[m % 2][:, 0:512], sem=sv)
                    self.wait("tensor", sv, v_p)
                    if m == 0:
                        self.wait("tensor", sv, getattr(self, "pc_last_vo", 0))
                    for tt in range(4):
                        dstp = (PSA if tt < 2 else PSB)[:, (tt % 2) * 1024 + m * 128:
                                                        (tt % 2) * 1024 + (m + 1) * 128]
                        tr_mm = self.tr(dstp, gs_[:, tt * 128:(tt + 1) * 128],
                                        self.identf[:, :],
                                        sem=(sm if tt == 3 else None))
                    tr_hist.append(tr_mm)
                    if m + 2 < 8:
                        self.wait("sync", sv, v_p)
                        ld_s[m + 2] = self.dma("sync", self.qss[m % 2], sst[m % 2][:, 0:512],
                                               self.d_sS[(m + 2) * 128:(m + 3) * 128,
                                                         c0g:c0g + 512])
                self.wait("vector", sm, tr_mm)
                vo_hist = []
                for tt in range(4):
                    tok0 = c0g + tt * 128
                    if len(vo_hist) >= 2:
                        self.wait("sync", sv, vo_hist[-2])
                    ldx2 = self.dma("sync", self.qx[tt % 2], xst[tt % 2],
                                    self.d_x[tok0:tok0 + 128, :])
                    self.wait("vector", self.qx[tt % 2], ldx2)
                    if len(out_stores) >= 2:
                        self.wait("vector", self.so_out[tt % 2], out_stores[-2])
                    v_o = self.v("tensor_add", ost[tt % 2], xst[tt % 2],
                                 (PSA if tt < 2 else PSB)[:, (tt % 2) * 1024:
                                                          (tt % 2) * 1024 + 1024], sem=sv)
                    vo_hist.append(v_o)
                    self.pc_last_vo = v_o
                    self.wait("scalar", sv, v_o)
                    out_stores.append(self.dma("scalar", self.so_out[tt % 2],
                                               self.d_out[tok0:tok0 + 128, :], ost[tt % 2]))


def _prep_weights(inp):
    f32 = np.float32
    W = {}
    W["W_in"] = np.ascontiguousarray(np.asarray(inp["L0_attn_W_in"], f32))
    W["W_ba"] = np.ascontiguousarray(np.asarray(inp["L0_attn_W_ba"], f32))
    W["cw_attn"] = np.ascontiguousarray(np.asarray(inp["L0_attn_conv_w"], f32)[:, 0, :].reshape(24, 128, 4))
    W["dtb"] = np.ascontiguousarray(np.asarray(inp["L0_attn_dt_bias"], f32).reshape(16, 1))
    W["dtb2"] = np.ascontiguousarray(np.broadcast_to(np.asarray(inp["L0_attn_dt_bias"], f32), (128, 16)))
    W["Alog"] = np.ascontiguousarray(np.asarray(inp["L0_attn_A_log"], f32).reshape(16, 1))
    W["Alog2"] = np.ascontiguousarray(np.broadcast_to(np.asarray(inp["L0_attn_A_log"], f32), (128, 16)))
    W["normw"] = np.ascontiguousarray(np.tile(np.asarray(inp["L0_attn_norm_w"], f32), 2).reshape(128, 1))
    W["W_o"] = np.ascontiguousarray(np.asarray(inp["L0_attn_W_o"], f32))
    W["cwf"] = np.ascontiguousarray(np.asarray(inp["L0_ffn_conv_w"], f32).transpose(2, 1, 0))
    W["cbf"] = np.ascontiguousarray(np.asarray(inp["L0_ffn_conv_b"], f32).reshape(8, 128).T)
    W["wqkvT"] = np.ascontiguousarray(np.asarray(inp["L0_ffn_wqkv"], f32).T)
    W["bqkv"] = np.ascontiguousarray(np.asarray(inp["L0_ffn_bqkv"], f32).reshape(24, 128).T)
    W["woT"] = np.ascontiguousarray(np.asarray(inp["L0_ffn_wo"], f32).T)
    W["bo"] = np.ascontiguousarray(np.asarray(inp["L0_ffn_bo"], f32).reshape(8, 128).T)
    W["ln1g"] = np.ascontiguousarray(np.asarray(inp["L0_ffn_ln1_g"], f32).reshape(8, 128).T)
    W["ln1b"] = np.ascontiguousarray(np.asarray(inp["L0_ffn_ln1_b"], f32).reshape(8, 128).T)
    W["ln2g"] = np.ascontiguousarray(np.asarray(inp["L0_ffn_ln2_g"], f32).reshape(8, 128).T)
    W["ln2b"] = np.ascontiguousarray(np.asarray(inp["L0_ffn_ln2_b"], f32).reshape(8, 128).T)
    W["gatew"] = np.ascontiguousarray(np.asarray(inp["L0_ffn_gate_w"], f32))
    W["gateb"] = np.ascontiguousarray(np.asarray(inp["L0_ffn_gate_b"], f32).reshape(8, 1))
    W["w1"] = np.ascontiguousarray(
        np.asarray(inp["L0_ffn_w1"], f32).reshape(NE, 8, 128, 32, 128).transpose(0, 3, 2, 1, 4))
    W["b1"] = np.ascontiguousarray(np.asarray(inp["L0_ffn_b1"], f32).reshape(NE, 32, 128).transpose(0, 2, 1))
    W["w2"] = np.ascontiguousarray(
        np.asarray(inp["L0_ffn_w2"], f32).reshape(NE, 32, 128, 2, 512).transpose(0, 1, 3, 2, 4))
    W["b2"] = np.ascontiguousarray(np.asarray(inp["L0_ffn_b2"], f32))
    W["identb"] = np.eye(128, dtype=BF_NP)
    W["identf"] = np.eye(128, dtype=np.float32)
    s_idx = np.arange(128)[:, None]
    t_idx = np.arange(128)[None, :]
    W["maskS"] = np.where(s_idx < t_idx, 0.0, -1e30).astype(f32)
    W["maskI"] = np.where(s_idx <= t_idx, 0.0, -1e30).astype(f32)
    W["maskST"] = np.where(s_idx > t_idx, 0.0, -1e30).astype(f32)
    sel = np.zeros((16, 16 * 128), f32)
    for h in range(16):
        sel[h, 128 * h:128 * (h + 1)] = 1.0
    W["sel16"] = sel
    sel8 = np.zeros((8, 8 * 128), BF_NP)
    for e in range(8):
        sel8[e, 128 * e:128 * (e + 1)] = 1.0
    W["sel8b"] = sel8
    W["triu"] = (s_idx <= t_idx).astype(f32)
    e2 = np.zeros((128, 2), BF_NP)
    e2[0:64, 0] = 1
    e2[64:128, 1] = 1
    W["e2"] = e2
    sel2 = np.zeros((2, 128), BF_NP)
    sel2[0, 0:64] = 1
    sel2[1, 64:128] = 1
    W["sel2"] = sel2
    W["onesb"] = np.ones((128, 128), BF_NP)
    return W


_PROG_CACHE = {}


def kernel(**inputs):
    T = 4096
    key = (T, False)
    if key not in _PROG_CACHE:
        _PROG_CACHE[key] = KB(T=T, debug=False).build()
    nc = _PROG_CACHE[key]
    W = _prep_weights(inputs)
    x = np.ascontiguousarray(np.asarray(inputs["L0_attn_x"], np.float32))
    in_maps = []
    for b in range(8):
        m = dict(W)
        m["x"] = np.ascontiguousarray(x[b])
        in_maps.append(m)
    res = run_bass_kernel_spmd(nc, in_maps, core_ids=list(range(8)))
    return np.stack([res.results[b]["out"] for b in range(8)], axis=0)
